# version 08
# baseline (speedup 1.0000x reference)
import os
from contextlib import ExitStack

import numpy as np
import ml_dtypes
import concourse.bass as bass
import concourse.mybir as mybir
from concourse.bass_utils import run_bass_kernel_spmd

F32 = mybir.dt.float32
BF16 = mybir.dt.bfloat16
AF = mybir.ActivationFunctionType
ALU = mybir.AluOpType
AX = mybir.AxisListType

BATCH, S, D, H, DH, E, HID = 16, 2048, 2048, 16, 128, 8, 1024
NCORES = 8
BPC = BATCH // NCORES
NT = S // 128
NDT = D // 128
NJT = HID // 128
TC = 1024
NCHUNK = (BPC * S) // TC
NE = E + 1
ATTN_SCALE = np.float32(np.sqrt(np.float32(DH)))
CINV = float(np.float32(1.0) / ATTN_SCALE)
SQRT_DH = float(ATTN_SCALE)
LN_EPS = 1e-5
MASK_NEG = -2.0e6

ENG = ("sync", "gpsimd", "tensor", "vector", "scalar")
DMAS = ("dS", "dC", "dR", "dW", "dX", "dX2", "dPI0", "dPI1", "dPO0", "dPO1",
        "dSP", "dF0", "dF1", "dW2", "dOA", "dOB")
SEMS = DMAS + ("tS", "vS", "sS", "gS")
ESEM = {"tensor": "tS", "vector": "vS", "scalar": "sS", "gpsimd": "gS"}


class Sched:
    def __init__(self):
        self.ops = []
        self.count = {s: 0 for s in SEMS}

    def dma(self, queue, dst, src, waits=(), sem="dS"):
        self.count[sem] += 16
        tok = (sem, self.count[sem])
        self.ops.append((queue, _mx(waits),
                         (lambda eng, dst=dst, src=src: eng.dma_start(dst, src)),
                         sem, 16))
        return tok

    def op(self, engine, emit_fn, waits=(), inc=False):
        sem = ESEM[engine] if inc else None
        tok = None
        if sem:
            self.count[sem] += 1
            tok = (sem, self.count[sem])
        self.ops.append((engine, _mx(waits), emit_fn, sem, 1 if sem else 0))
        return tok

    def barrier_token(self):
        return tuple((s, v) for s, v in self.count.items() if v > 0)

    def check_deadlock(self):
        qs = {e: [] for e in ENG}
        for i, (eng, waits, _f, sem, amt) in enumerate(self.ops):
            qs[eng].append((i, waits, sem, amt))
        heads = {e: 0 for e in ENG}
        sem_val = {s: 0 for s in SEMS}
        progress = True
        while progress:
            progress = False
            for e in ENG:
                while heads[e] < len(qs[e]):
                    _i, waits, sem, amt = qs[e][heads[e]]
                    if all(sem_val[s] >= v for s, v in waits):
                        if sem:
                            sem_val[sem] += amt
                        heads[e] += 1
                        progress = True
                    else:
                        break
        stuck = {e: len(qs[e]) - heads[e] for e in ENG if heads[e] < len(qs[e])}
        if stuck:
            detail = []
            for e in ENG:
                if heads[e] < len(qs[e]):
                    i, waits, _s, _a = qs[e][heads[e]]
                    unmet = [(s, v, sem_val[s]) for s, v in waits if sem_val[s] < v]
                    detail.append(f"{e}@op{i} needs {unmet}")
            raise RuntimeError(f"schedule deadlock: {stuck}; " + "; ".join(detail))

    def replay(self, name, eng, sems):
        for engine, waits, emit_fn, sem, amt in self.ops:
            if engine != name:
                continue
            for s, v in waits:
                eng.wait_ge(sems[s], v)
            inst = emit_fn(eng)
            if sem:
                inst.then_inc(sems[sem], amt)


def _mx(*tokens):
    best = {}
    for t in tokens:
        if not t:
            continue
        if isinstance(t[0], str):
            t = (t,)
        for s, v in t:
            if v > best.get(s, -1):
                best[s] = v
    return tuple(best.items())


def build_program(stage=3):
    nc = bass.Bass("TRN2", target_bir_lowering=False, debug=False)

    x_in = nc.dram_tensor("x", [BPC, S, D], F32, kind="ExternalInput").ap()
    wq_in = nc.dram_tensor("wq", [H, 128, NDT, DH], F32, kind="ExternalInput").ap()
    wk_in = nc.dram_tensor("wk", [H, 128, NDT, DH], F32, kind="ExternalInput").ap()
    wv_in = nc.dram_tensor("wv", [H, 128, NDT, DH], F32, kind="ExternalInput").ap()
    wo_in = nc.dram_tensor("wo", [H, 128, D], F32, kind="ExternalInput").ap()
    bqkv_in = nc.dram_tensor("bqkv", [128, 48], F32, kind="ExternalInput").ap()
    bo_in = nc.dram_tensor("bo", [1, D], F32, kind="ExternalInput").ap()
    gw_in = nc.dram_tensor("gw", [128, NDT * E], F32, kind="ExternalInput").ap()
    eb_in = nc.dram_tensor("eb", [1, E], F32, kind="ExternalInput").ap()
    w13_in = nc.dram_tensor("w13", [NE, NJT, 128, 4096], F32, kind="ExternalInput").ap()
    w2_in = nc.dram_tensor("w2", [NE, 128, NJT * D], F32, kind="ExternalInput").ap()
    y_out = nc.dram_tensor("y", [BPC, S, D], F32, kind="ExternalOutput").ap()
    dbg_mode = os.environ.get("MK_DBG", "")
    dbg_out = (nc.dram_tensor("dbg", [128, 32768], BF16, kind="ExternalOutput").ap()
               if dbg_mode else None)

    w13s = nc.dram_tensor("w13s", [NE, NJT, 128, 4096], BF16, kind="Internal").ap()
    w2s = nc.dram_tensor("w2s", [NE, 128, NJT * D], BF16, kind="Internal").ap()
    x1s = nc.dram_tensor("x1s", [BPC * NT, 128, D], F32, kind="Internal").ap()

    ident_bf = nc.inline_tensor(np.eye(128, dtype=ml_dtypes.bfloat16), name="idbf").ap()
    ident_f = nc.inline_tensor(np.eye(128, dtype=np.float32), name="idf").ap()
    tril = np.tril(np.ones((128, 128), bool))
    minfl = nc.inline_tensor(
        np.where(tril, np.float32(3.0e38), np.float32(MASK_NEG)).astype(np.float32),
        name="minfl").ap()
    minfu = nc.inline_tensor(
        np.where(tril.T, np.float32(3.0e38), np.float32(MASK_NEG)).astype(np.float32),
        name="minfu").ap()
    ones1 = nc.inline_tensor(np.ones((1, 128), np.float32), name="ones1").ap()
    rev_c = nc.inline_tensor(
        np.tile(np.arange(E, 0, -1, dtype=np.float32)[None, :], (128, 1)),
        name="revc").ap()
    ones2_c = nc.inline_tensor(np.ones((2, 128), ml_dtypes.bfloat16), name="ones2").ap()
    sel8_np = np.zeros((8, 8, 128), ml_dtypes.bfloat16)
    for _e in range(8):
        sel8_np[_e, _e, :] = 1.0
    sel8_c = nc.inline_tensor(np.ascontiguousarray(
        sel8_np.transpose(1, 0, 2).reshape(8, 1024)), name="sel8").ap()
    eps_c = nc.inline_tensor(np.full((128, 1), LN_EPS, np.float32), name="epsc").ap()

    sch = Sched()
    stk = ExitStack()
    sb = lambda name, shape, dt: stk.enter_context(nc.sbuf_tensor(name, shape, dt))

    with stk:
        big = sb("big", [128, 32768], BF16)
        zreg = sb("zreg", [128, 32768], BF16)
        qkvr = sb("qkvr", [128, 4096], F32)
        fstage = sb("fstage", [128, 2048], F32)
        wbf = sb("wbf", [128, 4096], BF16)
        xst = sb("xst", [128, 2048], F32)
        pmpi = sb("pmpi", [128, 2048], F32)
        pmpo = sb("pmpo", [128, 2048], BF16)
        vtm2 = sb("vtm2", [128, 2048], BF16)
        mbc = sb("mbc", [128, 1024], F32)
        ebuf = sb("ebuf", [128, 1024], BF16)
        junkb = sb("junkb", [128, 512], BF16)
        stats = sb("stats", [128, 64], F32)
        row1 = sb("row1", [1, 1024], F32)
        gts = sb("gts", [8, TC], BF16)
        sel8_s = sb("sel8_s", [8, 1024], BF16)
        ones2_s = sb("ones2_s", [2, 128], BF16)
        g8 = sb("g8", [128, 64], F32)
        idb_s = sb("idb_s", [128, 128], BF16)
        idf_s = sb("idf_s", [128, 128], F32)
        mfl_s = sb("mfl_s", [128, 128], F32)
        mfu_s = sb("mfu_s", [128, 128], F32)
        ones_s = sb("ones_s", [1, 128], F32)
        bqkv_s = sb("bqkv_s", [128, 48], F32)
        bo_bf = sb("bo_bf", [128, 2048], BF16)
        gw_s = sb("gw_s", [128, 128], F32)
        eb_s = sb("eb_s", [128, 8], F32)
        rev_s = sb("rev_s", [128, 8], F32)
        eps_s = sb("eps_s", [128, 1], F32)
        arena = stk.enter_context(nc.psum_tensor("arena", [128, 4096], F32))

        sems = {s: stk.enter_context(nc.semaphore(s)) for s in SEMS}
        blk = stk.enter_context(nc.Block())

        arena_ap = arena[:]
        arena_bf = arena_ap.bitcast(BF16)

        def bank(i, w=512, off=0):
            return arena_ap[:, i * 512 + off: i * 512 + off + w]

        def bank_bf(i, w=1024, off=0):
            return arena_bf[:, i * 1024 + off: i * 1024 + off + w]

        qkv_bf = qkvr[:].bitcast(BF16)
        mbc_bf = mbc[:].bitcast(BF16)
        def echunk(epar, wdt=512):
            if epar < 2:
                return ebuf[:, epar * 512: epar * 512 + wdt]
            return mbc_bf[:, (epar - 2) * 512: (epar - 2) * 512 + wdt]
        qh = qkv_bf[:, 0:2048]
        kh = qkv_bf[:, 2048:4096]
        vfm = qkv_bf[:, 4096:6144]
        vtm = qkv_bf[:, 6144:8192]
        xst_bf = None
        big_a16 = big[:].rearrange("p (a b) -> p a b", a=16)
        big_a32 = big[:].rearrange("p (a b) -> p a b", a=32)

        T = lambda f, w=(), inc=False: sch.op("tensor", f, w, inc)
        V = lambda f, w=(), inc=False: sch.op("vector", f, w, inc)
        SC = lambda f, w=(), inc=False: sch.op("scalar", f, w, inc)
        DMA = sch.dma

        for _dst, _src in ((idb_s[:], ident_bf), (idf_s[:], ident_f),
                           (mfl_s[:], minfl), (mfu_s[:], minfu),
                           (ones_s[:], ones1), (bqkv_s[:], bqkv_in),
                           (rev_s[:], rev_c), (gw_s[:], gw_in),
                           (eps_s[:], eps_c), (sel8_s[:], sel8_c),
                           (ones2_s[:], ones2_c)):
            DMA("sync", _dst, _src, sem="dC")
        tCN = (("dC", sch.count["dC"]),)

        tk_bo = []
        prev_cp = ()
        for hh in range(2):
            tdr = DMA("sync", row1[0:1, :], bo_in[0:1, hh * 1024:(hh + 1) * 1024],
                      waits=_mx(prev_cp), sem="dR")
            for ch in range(2):
                tmm = T(lambda e, hh=hh, ch=ch: e.matmul(
                    bank(6), ones_s[0:1, :], row1[0:1, ch * 512:(ch + 1) * 512],
                    start=True, stop=True), w=_mx(tdr, prev_cp, tCN), inc=True)
                prev_cp = V(lambda e, hh=hh, ch=ch: e.tensor_copy(
                    bo_bf[:, hh * 1024 + ch * 512: hh * 1024 + (ch + 1) * 512],
                    bank(6)), w=(tmm,), inc=True)
                tk_bo.append(prev_cp)
        tdr = DMA("sync", row1[0:1, 0:8], eb_in, waits=_mx(prev_cp), sem="dR")
        tmm = T(lambda e: e.matmul(bank(6, 8), ones_s[0:1, :], row1[0:1, 0:8],
                                   start=True, stop=True),
                w=_mx(tdr, prev_cp), inc=True)
        tEB = V(lambda e: e.tensor_copy(eb_s[:, 0:8], bank(6, 8)), w=(tmm,), inc=True)
        tBO = _mx(*tk_bo)
        tk_row1_free = tEB
        tk_bank = {i: () for i in range(8)}
        tk_bank[6] = _mx(tEB, tBO)

        pump_steps = []
        for ee in range(NE):
            for jt in range(NJT):
                for qq in range(4):
                    pump_steps.append((w13_in[ee, jt, :, qq * 1024:(qq + 1) * 1024],
                                       w13s[ee, jt, :, qq * 1024:(qq + 1) * 1024],
                                       ("w13", ee, jt)))
            for qq in range(16):
                pump_steps.append((w2_in[ee, :, qq * 1024:(qq + 1) * 1024],
                                   w2s[ee, :, qq * 1024:(qq + 1) * 1024],
                                   ("w2", ee)))
        pump_tok = {}
        pstate = {"i": 0, "in": [(), ()], "cast": [(), ()], "out": [(), ()]}

        def pump_issue_in(i):
            if i >= len(pump_steps):
                return
            src = pump_steps[i][0]
            par = i % 2
            pstate["in"][par] = DMA("sync", pmpi[:, par * 1024:(par + 1) * 1024],
                                    src, waits=_mx(pstate["cast"][par]),
                                    sem="dPI%d" % par)

        def pump_tick():
            i = pstate["i"]
            if i >= len(pump_steps):
                return
            pstate["i"] = i + 1
            if i == 0:
                pump_issue_in(0)
                pump_issue_in(1)
            par = i % 2
            dst, key = pump_steps[i][1], pump_steps[i][2]
            tcast = SC(lambda e, par=par: e.activation(
                pmpo[:, par * 1024:(par + 1) * 1024],
                pmpi[:, par * 1024:(par + 1) * 1024], AF.Copy),
                w=_mx(pstate["in"][par], pstate["out"][par]), inc=True)
            pstate["cast"][par] = tcast
            tout = DMA("sync", dst, pmpo[:, par * 1024:(par + 1) * 1024],
                       waits=(tcast,), sem="dPO%d" % par)
            pstate["out"][par] = tout
            pump_tok[key] = tout
            pump_issue_in(i + 2)

        def pump_flush():
            while pstate["i"] < len(pump_steps):
                pump_tick()

        def emit_ln(xsrc, wx, par, junk_guard=(), tm_guard=(), want_tm=True):
            c0, c1 = 24 + 3 * par, 25 + 3 * par
            tsum = V(lambda e: e.reduce_sum(stats[:, c0:c0 + 1], xsrc, axis=AX.X),
                     w=wx, inc=True)
            tneg = V(lambda e: e.tensor_scalar_mul(stats[:, c0:c0 + 1],
                                                   stats[:, c0:c0 + 1], -1.0 / D),
                     w=(tsum,), inc=True)
            tsq = SC(lambda e: e.activation(wbf[:, 0:2048], xsrc, AF.Square,
                                            bias=stats[:, c0:c0 + 1],
                                            accum_out=stats[:, c1:c1 + 1]),
                     w=_mx(tneg, wx, junk_guard), inc=True)
            tsd = SC(lambda e: e.activation(stats[:, c1:c1 + 1], stats[:, c1:c1 + 1],
                                            AF.Sqrt, bias=eps_s[:, 0:1], scale=1.0 / D),
                     w=(tsq,), inc=True)
            trs = V(lambda e: e.reciprocal(stats[:, c1:c1 + 1], stats[:, c1:c1 + 1]),
                    w=(tsd,), inc=True)
            ttm = None
            if want_tm:
                ttm = V(lambda e: e.tensor_scalar(wbf[:, 2048:4096], xsrc,
                                                  stats[:, c0:c0 + 1],
                                                  stats[:, c1:c1 + 1],
                                                  ALU.add, ALU.mult),
                        w=_mx(trs, tm_guard), inc=True)
            return {"sum": tsum, "neg": tneg, "sq": tsq, "sd": tsd, "rs": trs,
                    "tm": ttm, "c0": c0, "c1": c1}

        tk_bankw = {i: () for i in range(8)}
        tk_wbf_free = [(), ()]
        tk_fstage_free = ()
        tk_xst_free = ()
        tk_qkv_free = {"q": (), "k": (), "vf": (), "vt": ()}
        tk_ebuf = [(), (), (), ()]
        tk_stats_par = [(), ()]
        tk_mbc_free = ()
        tk_junk_guard = ()
        tk_tm_guard = ()
        tk_x1 = {}
        tk_z_done = {}

        tk_parity_free = {0: (), 1: ()}
        for b in range(BPC):
            tk_junk_guard = _mx(tk_junk_guard, tk_wbf_free[0])
            tk_tm_guard = _mx(tk_tm_guard, tk_wbf_free[1])
            tk_xln = ()
            for tt in range(NT):
                wx = _mx(DMA("sync", xst[:], x_in[b, tt * 128:(tt + 1) * 128, :],
                             waits=tk_xst_free, sem="dX"))
                ln = emit_ln(xst[:], wx, tt % 2, junk_guard=tk_junk_guard,
                             tm_guard=tk_tm_guard)
                tk_junk_guard = ln["sq"]
                tk_xst_free = ln["tm"]
                tr_last = ()
                for r in range(2):
                    bk = 6 + r
                    tks = []
                    for i in range(8):
                        dt = r * 8 + i
                        tks.append(T(lambda e, dt=dt, i=i, bk=bk: e.transpose(
                            bank_bf(bk, 128, i * 128),
                            wbf[:, 2048 + dt * 128: 2048 + (dt + 1) * 128],
                            idb_s[:]),
                            w=_mx(ln["tm"], tk_bank[bk], tCN) if i == 0 else (),
                            inc=(i == 7)))
                    tk_bankw[bk] = tks[-1]
                    tev = V(lambda e, r=r, tt=tt, bk=bk: e.tensor_copy(
                        big_a16[:, r * 8:(r + 1) * 8, tt * 128:tt * 128 + 128],
                        bank_bf(bk).rearrange("p (a b) -> p a b", a=8)),
                        w=(tks[-1],), inc=True)
                    tk_bank[bk] = tev
                    tk_xln = _mx(tk_xln, tev)
                    tr_last = _mx(tr_last, tks[-1])
                tk_tm_guard = tr_last
                pump_tick()

            qk2 = xst[:].bitcast(BF16)
            hstate = {}
            tk_g8_free = {0: (), 1: ()}
            sc_rot = {"i": 0}
            tk_xln_reads = ()
            def next_bk():
                sc_rot["i"] = (sc_rot["i"] + 1) % 4
                return sc_rot["i"]
            tk_xln_cur = tk_xln

            def gen_fills(h):
                nonlocal tk_fstage_free, tk_xln_reads
                par = h % 2
                if par == 0:
                    bq_, bk_, bvt = qh, kh, vtm
                else:
                    bq_, bk_, bvt = qk2[:, 0:2048], qk2[:, 2048:4096], vtm2[:]
                free_tok = tk_parity_free[par]
                projs = (("q", wq_in, 0, bq_, "bq"), ("k", wk_in, 1, bk_, "bk"),
                         ("vf", wv_in, 0, vfm, None))
                tk_fill = {}
                for nm, w_mat, slot, dstbuf, biasnm in projs:
                    wdma = DMA("sync", fstage[:],
                               w_mat[h].rearrange("p a b -> p (a b)"),
                               waits=_mx(tk_fstage_free), sem="dW")
                    guard = tk_junk_guard if slot == 0 else tk_tm_guard
                    tcast = SC(lambda e, slot=slot: e.activation(
                        wbf[:, slot * 2048:(slot + 1) * 2048], fstage[:], AF.Copy),
                        w=_mx(wdma, tk_wbf_free[slot], guard), inc=True)
                    tk_fstage_free = tcast
                    buf_guard = free_tok if nm in ("q", "k") else tk_qkv_free["vf"]
                    fill_last = ()
                    fill_mm_last = ()
                    for chn in range(4):
                        bk = 6 + chn % 2
                        wfirst = _mx(tcast, tk_bank[bk], tk_xln_cur, buf_guard)
                        tks = []
                        for dt in range(NDT):
                            tks.append(T(lambda e, dt=dt, chn=chn, slot=slot, bk=bk:
                                e.matmul(bank(bk),
                                         wbf[:, slot * 2048 + dt * 128:
                                             slot * 2048 + (dt + 1) * 128],
                                         big[:, dt * 2048 + chn * 512:
                                             dt * 2048 + chn * 512 + 512],
                                         start=(dt == 0), stop=(dt == NDT - 1)),
                                w=wfirst if dt == 0 else (), inc=(dt == NDT - 1)))
                        tk_bankw[bk] = tks[-1]
                        fill_mm_last = _mx(fill_mm_last, tks[-1])
                        if biasnm is not None:
                            bcol = {"bq": 0, "bk": 16}[biasnm] + h
                            tev = V(lambda e, chn=chn, dstbuf=dstbuf, bcol=bcol, bk=bk:
                                e.tensor_scalar_add(
                                    dstbuf[:, chn * 512:(chn + 1) * 512],
                                    bank(bk), bqkv_s[:, bcol:bcol + 1]),
                                w=(tks[-1],), inc=True)
                        else:
                            tev = V(lambda e, chn=chn, dstbuf=dstbuf, bk=bk:
                                e.tensor_copy(dstbuf[:, chn * 512:(chn + 1) * 512],
                                              bank(bk)),
                                w=(tks[-1],), inc=True)
                        tk_bank[bk] = _mx(tk_bank[bk], tev)
                        fill_last = _mx(fill_last, tev)
                        yield
                    tk_fill[nm] = fill_last
                    tk_wbf_free[slot] = fill_mm_last
                    pump_tick()
                tvt = ()
                for r in range(2):
                    bk = 6 + r
                    tks = []
                    for i in range(8):
                        tt2 = r * 8 + i
                        tks.append(T(lambda e, tt2=tt2, i=i, bk=bk: e.transpose(
                            bank_bf(bk, 128, i * 128),
                            vfm[:, tt2 * 128:(tt2 + 1) * 128], idb_s[:]),
                            w=_mx(tk_fill["vf"], tk_bank[bk],
                                  free_tok) if i == 0 else (),
                            inc=(i == 7)))
                    tk_bankw[bk] = tks[-1]
                    tev = V(lambda e, r=r, bk=bk, bvt=bvt: e.tensor_copy(
                        bvt[:, r * 1024:(r + 1) * 1024], bank_bf(bk)),
                        w=(tks[-1],), inc=True)
                    tk_bank[bk] = tev
                    tvt = _mx(tvt, tev)
                    yield
                tk_qkv_free["vf"] = tvt
                tk_xln_reads = _mx(tk_xln_reads, tk_fill["q"], tk_fill["k"],
                                   tk_fill["vf"])
                hstate[h] = dict(fq=tk_fill["q"], fk=tk_fill["k"], tvt=tvt,
                                 q=bq_, k=bk_, vt=bvt)

            def gen_pass1(h):
                st = hstate[h]
                bq_, bk_ = st["q"], st["k"]
                goff = 16 * (h % 2)
                t_m2c = []
                for qt in range(NT):
                    par = qt % 2
                    so = 32 * par
                    kend = (qt + 1) * 128
                    nch = (kend + 511) // 512
                    t_sum_toks = []
                    t_m_toks = []
                    guard = tk_stats_par[par]
                    for ch in range(nch):
                        lo = ch * 512
                        wdt = min(512, kend - lo)
                        bk = next_bk()
                        tmm = T(lambda e, qt=qt, lo=lo, wdt=wdt, bk=bk, bq_=bq_, bk_=bk_:
                            e.matmul(bank(bk, wdt), bq_[:, qt * 128:(qt + 1) * 128],
                                     bk_[:, lo:lo + wdt], start=True, stop=True),
                            w=_mx(st["fq"], st["fk"], tk_bank[bk]), inc=True)
                        tk_bankw[bk] = tmm
                        tpre = tmm
                        if lo + wdt == kend:
                            tpre = V(lambda e, bk=bk, wdt=wdt: e.tensor_tensor(
                                bank(bk, 128, wdt - 128), bank(bk, 128, wdt - 128),
                                mfl_s[:], ALU.min), w=_mx(tmm, tCN), inc=True)
                        tmax = V(lambda e, bk=bk, wdt=wdt, so=so, ch=ch: e.reduce_max(
                            stats[:, so + ch:so + ch + 1], bank(bk, wdt), axis=AX.X),
                            w=_mx(tpre, guard), inc=True)
                        tngm = SC(lambda e, so=so, ch=ch: e.mul(
                            stats[:, so + 10 + ch:so + 11 + ch],
                            stats[:, so + ch:so + ch + 1], -CINV),
                            w=_mx(tmax, guard), inc=True)
                        texp = SC(lambda e, bk=bk, wdt=wdt, so=so, ch=ch: e.activation(
                            junkb[:, 0:wdt], bank(bk, wdt), AF.Exp,
                            bias=stats[:, so + 10 + ch:so + 11 + ch], scale=CINV,
                            accum_out=stats[:, so + 4 + ch:so + 5 + ch]),
                            w=(tngm,), inc=True)
                        tk_bank[bk] = _mx(tk_bank[bk], texp)
                        t_sum_toks.append(texp)
                        t_m_toks.append(tmax)
                        yield
                    tm = V(lambda e, so=so, nch=nch: e.reduce_max(
                        stats[:, so + 8:so + 9], stats[:, so:so + nch], axis=AX.X),
                        w=_mx(*t_m_toks), inc=True)
                    tnm = SC(lambda e, so=so: e.mul(
                        stats[:, so + 9:so + 10], stats[:, so + 8:so + 9], -CINV),
                        w=(tm,), inc=True)
                    tec = SC(lambda e, so=so, nch=nch: e.activation(
                        stats[:, so + 14:so + 14 + nch], stats[:, so:so + nch],
                        AF.Exp, bias=stats[:, so + 9:so + 10], scale=CINV),
                        w=_mx(tnm, *t_sum_toks), inc=True)
                    tpr = V(lambda e, so=so, nch=nch: e.tensor_tensor(
                        stats[:, so + 18:so + 18 + nch],
                        stats[:, so + 4:so + 4 + nch],
                        stats[:, so + 14:so + 14 + nch], ALU.mult),
                        w=(tec,), inc=True)
                    tsm = V(lambda e, so=so, nch=nch: e.reduce_sum(
                        stats[:, so + 22:so + 23], stats[:, so + 18:so + 18 + nch],
                        axis=AX.X), w=(tpr,), inc=True)
                    tln = SC(lambda e, so=so: e.activation(
                        stats[:, so + 23:so + 24], stats[:, so + 22:so + 23], AF.Ln),
                        w=(tsm,), inc=True)
                    tmc = V(lambda e, so=so, qt=qt, goff=goff: e.tensor_scalar(
                        g8[:, goff + qt:goff + qt + 1], stats[:, so + 23:so + 24],
                        SQRT_DH, stats[:, so + 8:so + 9], ALU.mult, ALU.add),
                        w=_mx(tln, tk_g8_free[h % 2] if qt < 2 else ()), inc=True)
                    tk_stats_par[par] = _mx(tln, tmc)
                    t_m2c.append(tmc)
                    pump_tick()
                    yield
                hstate[h]["m2c"] = t_m2c


            def gen_pass2(h):
                nonlocal tk_row1_free, tk_mbc_free
                st = hstate[h]
                bq_, bk_, bvt = st["q"], st["k"], st["vt"]
                goff = 16 * (h % 2)
                t_m2c = hstate[h]["m2c"]
                t_z_evac = ()
                ech = 0
                for half in range(2):
                    ttrs = []
                    for j in range(8):
                        qt = half * 8 + j
                        bkj = 2 + j // 4
                        ttrs.append(T(lambda e, j=j, qt=qt, goff=goff: e.transpose(
                            arena_ap[0:1, 1024 + j * 128: 1024 + (j + 1) * 128],
                            g8[:, goff + qt:goff + qt + 1], idf_s[:]),
                            w=_mx(t_m2c[qt], tk_bank[bkj] if j % 4 == 0 else (),
                                  tCN),
                            inc=(j in (3, 7))))
                    tk_bankw[2] = ttrs[3]
                    tk_bankw[3] = ttrs[7]
                    tk_g8_free[h % 2] = _mx(tk_g8_free[h % 2], ttrs[7])
                    thi = V(lambda e: e.tensor_scalar_mul(
                        mbc_bf[0:1, 1024:2048], arena_ap[0:1, 1024:2048], -1.0),
                        w=_mx(ttrs[3], ttrs[7], tk_row1_free), inc=True)
                    tlo = V(lambda e: e.scalar_tensor_tensor(
                        gts[0:1, 0:1024], arena_ap[0:1, 1024:2048], -1.0,
                        mbc_bf[0:1, 1024:2048], ALU.mult, ALU.subtract),
                        w=(thi,), inc=True)
                    tk_bank[2] = _mx(tk_bank[2], tlo)
                    tk_bank[3] = _mx(tk_bank[3], tlo)
                    tdl = DMA("sync", mbc_bf[1:2, 1024:2048], gts[0:1, 0:1024],
                              waits=(tlo,), sem="dR")
                    tMB = _mx(tlo, tdl)
                    tk_row1_free = ()
                    t_mbc_used = ()
                    t_z_last = ()
                    kmax = 8 if half == 0 else 16
                    chunks = []
                    for kt in range(kmax):
                        glo = max(kt * 128, half * 1024)
                        llo = glo - half * 1024
                        cuts = sorted(set([llo] + ([512] if llo < 512 else []) + [1024]))
                        first = True
                        for ci in range(len(cuts) - 1):
                            chunks.append((kt, cuts[ci], cuts[ci + 1] - cuts[ci],
                                           first and glo == kt * 128, kt == 0 and first))
                            first = False

                    pend = None
                    for (kt, lo, wdt, diag, zstart) in chunks:
                        bk = next_bk()
                        epar = ech % 4
                        ech += 1
                        tbias = T(lambda e, lo=lo, wdt=wdt, bk=bk: e.matmul(
                            bank(bk, wdt), ones2_s[0:2, :],
                            mbc_bf[0:2, 1024 + lo: 1024 + lo + wdt],
                            start=True, stop=False),
                            w=_mx(tk_bank[bk], tMB, tCN), inc=False)
                        tmm = T(lambda e, kt=kt, lo=lo, wdt=wdt, half=half, bk=bk,
                                bq_=bq_, bk_=bk_:
                            e.matmul(bank(bk, wdt),
                                     bk_[:, kt * 128:(kt + 1) * 128],
                                     bq_[:, half * 1024 + lo: half * 1024 + lo + wdt],
                                     start=False, stop=True),
                            w=_mx(st["fq"], st["fk"]),
                            inc=True)
                        tk_bankw[bk] = tmm
                        t_mbc_used = _mx(t_mbc_used, tmm)
                        tk_row1_free = _mx(tk_row1_free, tmm)
                        tpre = tmm
                        if diag:
                            tpre = V(lambda e, bk=bk: e.tensor_tensor(
                                bank(bk, 128), bank(bk, 128), mfu_s[:], ALU.min),
                                w=(tmm,), inc=True)
                        texp = SC(lambda e, bk=bk, wdt=wdt, epar=epar:
                            e.activation(echunk(epar, wdt),
                                         bank(bk, wdt), AF.Exp, scale=CINV),
                            w=_mx(tpre, tk_ebuf[epar]), inc=True)
                        tk_bank[bk] = _mx(tk_bank[bk], texp)
                        yield
                        if pend is not None:
                            (pkt, plo, pwdt, pzstart, ptexp, pepar) = pend
                            tzm = T(lambda e, kt=pkt, lo=plo, wdt=pwdt, epar=pepar,
                                    bvt=bvt:
                                e.matmul(arena_ap[:, 2048 + lo: 2048 + lo + wdt],
                                         bvt[:, kt * 128:(kt + 1) * 128],
                                         echunk(epar, wdt),
                                         start=(kt == 0), stop=False),
                                w=_mx(ptexp, st["tvt"],
                                      tk_bank[4] if pzstart else ()),
                                inc=True)
                            tk_ebuf[pepar] = tzm
                            tk_bankw[4] = tzm
                            t_z_last = _mx(t_z_last, tzm)
                        pend = (kt, lo, wdt, zstart, texp, epar)
                        if ech % 4 == 0:
                            pump_tick()
                        yield
                    if pend is not None:
                        (pkt, plo, pwdt, pzstart, ptexp, pepar) = pend
                        tzm = T(lambda e, kt=pkt, lo=plo, wdt=pwdt, epar=pepar,
                                bvt=bvt:
                            e.matmul(arena_ap[:, 2048 + lo: 2048 + lo + wdt],
                                     bvt[:, kt * 128:(kt + 1) * 128],
                                     echunk(epar, wdt),
                                     start=(kt == 0), stop=False),
                            w=_mx(ptexp, st["tvt"],
                                  tk_bank[4] if pzstart else ()),
                            inc=True)
                        tk_ebuf[pepar] = tzm
                        tk_bankw[4] = tzm
                        t_z_last = _mx(t_z_last, tzm)
                        pend = None
                    bvcol = 32 + h
                    tze = V(lambda e, h=h, half=half, bvcol=bvcol: e.tensor_scalar_add(
                        zreg[:, h * 2048 + half * 1024: h * 2048 + (half + 1) * 1024],
                        arena_ap[:, 2048:3072], bqkv_s[:, bvcol:bvcol + 1]),
                        w=_mx(t_z_last,), inc=True)
                    tk_bank[4] = tze
                    tk_bank[5] = tze
                    tk_mbc_free = t_mbc_used
                    t_z_evac = _mx(t_z_evac, tze)
                tk_z_done[(b, h)] = t_z_evac
                tk_parity_free[h % 2] = t_z_evac
                del hstate[h]

            def drive(lanes):
                live = [g for g in lanes if g is not None]
                while live:
                    for g in list(live):
                        try:
                            next(g)
                        except StopIteration:
                            live.remove(g)

            def lane_chain(*gens):
                for g in gens:
                    if g is not None:
                        yield from g

            drive([lane_chain(gen_fills(0), gen_pass1(0))])
            for h in range(H):
                lane2 = (lane_chain(gen_fills(h + 1), gen_pass1(h + 1))
                         if h + 1 < H else None)
                drive([gen_pass2(h), lane2])
            tk_xst_free = _mx(tk_xst_free, tk_parity_free[1])

            tk_allz = _mx(*[tk_z_done[(b, hh)] for hh in range(H)])
            tk_wo = ()
            for ht in range(H):
                wdma = DMA("sync", fstage[:], wo_in[ht], waits=tk_fstage_free,
                           sem="dW")
                tcast = SC(lambda e, ht=ht: e.activation(
                    big[:, ht * 2048:(ht + 1) * 2048], fstage[:], AF.Copy),
                    w=_mx(wdma, tk_xln_reads if ht == 0 else ()), inc=True)
                tk_fstage_free = tcast
                tk_wo = _mx(tk_wo, tcast)
                if ht % 4 == 0:
                    pump_tick()
            for qt in range(NT):
                ti = b * NT + qt
                wxt = DMA("sync", xst[:], x_in[b, qt * 128:(qt + 1) * 128, :],
                          waits=tk_xst_free, sem="dX")
                t_nch = ()
                for chn in range(4):
                    bk = 6 + chn % 2
                    tks = []
                    for ht in range(H):
                        tks.append(T(lambda e, qt=qt, chn=chn, ht=ht, bk=bk:
                            e.matmul(bank(bk),
                                     zreg[:, ht * 2048 + qt * 128:
                                          ht * 2048 + (qt + 1) * 128],
                                     big[:, ht * 2048 + chn * 512:
                                         ht * 2048 + chn * 512 + 512],
                                     start=(ht == 0), stop=(ht == H - 1)),
                            w=_mx(tk_wo, tk_allz, tk_bank[bk]) if ht == 0 else (),
                            inc=(ht == H - 1)))
                    tk_bankw[bk] = tks[-1]
                    tv1 = V(lambda e, chn=chn, bk=bk: e.tensor_tensor(
                        mbc[:, 0:512], bank(bk),
                        bo_bf[:, chn * 512:(chn + 1) * 512], ALU.add),
                        w=_mx(tks[-1], tBO, tk_mbc_free), inc=True)
                    tk_bank[bk] = tv1
                    tk_mbc_free = ()
                    tv2 = V(lambda e, chn=chn: e.tensor_tensor(
                        xst[:, chn * 512:(chn + 1) * 512],
                        xst[:, chn * 512:(chn + 1) * 512],
                        mbc[:, 0:512], ALU.add),
                        w=_mx(tv1, wxt), inc=True)
                    tk_mbc_free = tv2
                    t_nch = _mx(t_nch, tv2)
                tk_x1[ti] = DMA("sync", x1s[ti], xst[:], waits=t_nch, sem="dSP")
                tk_xst_free = tk_x1[ti]
                if stage == 1:
                    tk_xst_free = _mx(tk_x1[ti],
                                      DMA("sync", y_out[b, qt * 128:(qt + 1) * 128, :],
                                          xst[:], waits=t_nch))
                pump_tick()

        pump_flush()

        if stage >= 2:
            pump_final = (("dPO0", sch.count["dPO0"]), ("dPO1", sch.count["dPO1"]))
            spill_final = (("dSP", sch.count["dSP"]),)
            ne_run = int(os.environ.get("MK_NE", str(NE)))
            f1_only = os.environ.get("MK_F1ONLY", "0") == "1"
            f1lvl = int(os.environ.get("MK_F1LVL", "3"))
            barrier_A = sch.barrier_token()
            tk_h_free = ()
            tk_w13buf = [(), ()]
            tk_w2buf = ()
            tk_gbc_free = ()
            tk_t123_free = ()
            tk_gts_free = ()
            tk_x1a_free = ()
            tk_x1b_free = ()
            tk_fstage_free_F = ()
            t_acc_prev = {tt: () for tt in range(8)}

            for c in range(NCHUNK):
                t_ln_done = ()
                t_gts = ()
                for tt in range(8):
                    ti = c * 8 + tt
                    if tt % 2 == 0:
                        xbuf, xfree = xst, tk_x1a_free
                    else:
                        xbuf, xfree = pmpi, tk_x1b_free
                    wxt = DMA("sync", xbuf[:], x1s[ti],
                              waits=_mx(xfree, barrier_A, spill_final,
                                        tk_xst_free if (c == 0 and tt == 0) else ()),
                              sem="dX" if tt % 2 == 0 else "dX2")
                    ln = emit_ln(xbuf[:], (wxt,), tt % 2, junk_guard=tk_junk_guard,
                                 want_tm=False)
                    tk_junk_guard = ln["sq"]
                    tf32 = V(lambda e, xbuf=xbuf, c0=ln["c0"], c1=ln["c1"]:
                        e.tensor_scalar(xbuf[:], xbuf[:], stats[:, c0:c0 + 1],
                                        stats[:, c1:c1 + 1], ALU.add, ALU.mult),
                        w=(ln["rs"],), inc=True)
                    xbuf_last_read = tf32
                    tev_all = ()
                    for r in (range(4) if f1lvl >= 1 else range(0)):
                        bk = 6 + r % 2
                        tks = []
                        for i in range(4):
                            dt = r * 4 + i
                            tks.append(T(lambda e, dt=dt, i=i, bk=bk, xbuf=xbuf:
                                e.transpose(bank(bk, 128, i * 128),
                                            xbuf[:, dt * 128:(dt + 1) * 128],
                                            idf_s[:]),
                                w=_mx(tf32, tk_bank[bk], tCN) if i == 0 else (),
                                inc=(i == 3)))
                        tk_bankw[bk] = tks[-1]
                        tev = V(lambda e, r=r, tt=tt, bk=bk: e.tensor_copy(
                            big_a32[:, r * 4:(r + 1) * 4, tt * 128:tt * 128 + 128],
                            bank(bk).rearrange("p (a b) -> p a b", a=4)),
                            w=(tks[-1],), inc=True)
                        tev2 = SC(lambda e, r=r, bk=bk: e.activation(
                            fstage[:, r * 512:(r + 1) * 512], bank(bk), AF.Copy),
                            w=_mx(tks[-1], tev,
                                  tk_fstage_free_F if r == 0 else ()),
                            inc=True)
                        tk_bank[bk] = _mx(tev, tev2)
                        tev_all = _mx(tev_all, tev, tev2)
                        xbuf_last_read = _mx(xbuf_last_read, tks[-1])
                    t_ln_done = _mx(t_ln_done, tev_all)
                    if tt % 2 == 0:
                        tk_x1a_free = xbuf_last_read
                    else:
                        tk_x1b_free = xbuf_last_read
                    tks = []
                    for dt in (range(NDT) if f1lvl >= 2 else range(0)):
                        tks.append(T(lambda e, dt=dt: e.matmul(
                            bank(5, 8), fstage[:, dt * 128:(dt + 1) * 128],
                            gw_s[:, dt * 8:(dt + 1) * 8],
                            start=(dt == 0), stop=(dt == NDT - 1)),
                            w=_mx(tev_all, tk_bank[5], tCN) if dt == 0 else (),
                            inc=(dt == NDT - 1)))
                    if f1lvl < 2:
                        continue
                    tk_bankw[5] = tks[-1]
                    tk_fstage_free_F = tks[-1]
                    tsg = SC(lambda e: e.activation(g8[:, 16:24], bank(5, 8),
                                                    AF.Sigmoid),
                             w=_mx(tks[-1], tk_gts_free), inc=True)
                    tk_bank[5] = tsg
                    if f1lvl < 3:
                        tk_gts_free = tsg
                        continue
                    tb = V(lambda e: e.tensor_tensor(g8[:, 24:32], g8[:, 16:24],
                                                     eb_s[:, 0:8], ALU.add),
                           w=_mx(tsg, tEB), inc=True)
                    tm1 = V(lambda e: e.reduce_max(g8[:, 32:33], g8[:, 24:32],
                                                   axis=AX.X), w=(tb,), inc=True)
                    te1 = V(lambda e: e.tensor_scalar(g8[:, 40:48], g8[:, 24:32],
                                                      g8[:, 32:33], None,
                                                      ALU.is_equal),
                            w=(tm1,), inc=True)
                    tr1 = V(lambda e: e.tensor_tensor(g8[:, 40:48], g8[:, 40:48],
                                                      rev_s[:, 0:8], ALU.mult),
                            w=_mx(te1, tCN), inc=True)
                    tmr = V(lambda e: e.reduce_max(g8[:, 33:34], g8[:, 40:48],
                                                   axis=AX.X), w=(tr1,), inc=True)
                    tf1 = V(lambda e: e.tensor_scalar(g8[:, 40:48], g8[:, 40:48],
                                                      g8[:, 33:34], None,
                                                      ALU.is_equal),
                            w=(tmr,), inc=True)
                    tb2 = V(lambda e: e.scalar_tensor_tensor(g8[:, 48:56],
                                                             g8[:, 40:48], -1.0e9,
                                                             g8[:, 24:32],
                                                             ALU.mult, ALU.add),
                            w=(tf1,), inc=True)
                    tm2 = V(lambda e: e.reduce_max(g8[:, 34:35], g8[:, 48:56],
                                                   axis=AX.X), w=(tb2,), inc=True)
                    te2 = V(lambda e: e.tensor_scalar(g8[:, 48:56], g8[:, 48:56],
                                                      g8[:, 34:35], None,
                                                      ALU.is_equal),
                            w=(tm2,), inc=True)
                    tr2 = V(lambda e: e.tensor_tensor(g8[:, 48:56], g8[:, 48:56],
                                                      rev_s[:, 0:8], ALU.mult),
                            w=(te2,), inc=True)
                    tmr2 = V(lambda e: e.reduce_max(g8[:, 35:36], g8[:, 48:56],
                                                    axis=AX.X), w=(tr2,), inc=True)
                    tf2 = V(lambda e: e.tensor_scalar(g8[:, 48:56], g8[:, 48:56],
                                                      g8[:, 35:36], None,
                                                      ALU.is_equal),
                            w=(tmr2,), inc=True)
                    tmsk = V(lambda e: e.tensor_tensor(g8[:, 56:64], g8[:, 40:48],
                                                       g8[:, 48:56], ALU.max),
                             w=(tf2,), inc=True)
                    tgsc = V(lambda e: e.tensor_tensor(g8[:, 56:64], g8[:, 56:64],
                                                       g8[:, 16:24], ALU.mult),
                             w=(tmsk,), inc=True)
                    ttr = T(lambda e: e.transpose(arena_ap[0:8, 2560:2688],
                                                  g8[:, 56:64], idf_s[:]),
                            w=_mx(tgsc, tk_bank[5]), inc=True)
                    tk_bankw[5] = ttr
                    tgv = V(lambda e, tt=tt: e.tensor_copy(
                        gts[:, tt * 128:(tt + 1) * 128], arena_ap[0:8, 2560:2688]),
                        w=_mx(ttr, tk_gts_free), inc=True)
                    tk_bank[5] = tgv
                    t_gts = _mx(t_gts, tgv)

                if dbg_mode == "w13" and c == 0:
                    for _j in range(4):
                        DMA("sync", dbg_out[:, _j * 4096:(_j + 1) * 4096],
                            w13s[8, _j], waits=_mx(t_ln_done))
                if dbg_mode == "ln2" and c == 0:
                    DMA("sync", dbg_out[:, 0:16384], big[:, 0:16384],
                        waits=_mx(t_ln_done))
                t_gdma_prev = ()
                t_acc_last = dict(t_acc_prev)
                elist = (list(range(ne_run)) if not f1_only else [])
                if os.environ.get("MK_ELIST"):
                    elist = [int(t) for t in os.environ["MK_ELIST"].split(",")]
                for eidx, ei in enumerate(elist):
                    routed = ei < E
                    if routed:
                        tgb = []
                        for ch in range(2):
                            bk = 6 + ch
                            tmm = T(lambda e, ei=ei, ch=ch, bk=bk: e.matmul(
                                bank(bk), sel8_s[:, ei * 128:(ei + 1) * 128],
                                gts[0:8, ch * 512:(ch + 1) * 512],
                                start=True, stop=True),
                                w=_mx(t_gts, tk_bank[bk]), inc=True)
                            tk_bankw[bk] = tmm
                            t_gdma_prev = _mx(t_gdma_prev, tmm)
                            tcpg = V(lambda e, ch=ch, bk=bk: e.tensor_copy(
                                qkvr[:, 3072 + ch * 512: 3072 + (ch + 1) * 512],
                                bank(bk)), w=_mx(tmm, tk_gbc_free), inc=True)
                            tk_bank[bk] = tcpg
                            tgb.append(tcpg)
                        tGB = _mx(*tgb)
                        tk_gbc_free = ()
                    else:
                        tGB = ()
                    tw2 = DMA("sync", zreg[:, 16384:32768], w2s[ei],
                              waits=_mx(tk_w2buf, pump_final,
                                        barrier_A if (c == 0 and eidx == 0) else ()),
                              sem="dW2")
                    t_h_done = ()
                    for jt in range(NJT):
                        par = jt % 2
                        w13o = 24576 + par * 4096
                        twd = DMA("sync", big[:, w13o:w13o + 4096], w13s[ei, jt],
                                  waits=_mx(tk_w13buf[par], pump_final,
                                            barrier_A if (c == 0 and eidx == 0 and jt < 2) else ()),
                                  sem="dF%d" % par)
                        t_half = ()
                        for hf in range(2):
                            ubk = hf % 2
                            wk_ = 2 + hf % 2
                            tku = []
                            for dt in range(NDT):
                                tku.append(T(lambda e, dt=dt, hf=hf, ubk=ubk, w13o=w13o:
                                    e.matmul(bank(ubk),
                                             big[:, w13o + dt * 128:
                                                 w13o + (dt + 1) * 128],
                                             big[:, dt * 1024 + hf * 512:
                                                 dt * 1024 + hf * 512 + 512],
                                             start=(dt == 0), stop=(dt == NDT - 1)),
                                    w=_mx(twd, tk_bank[ubk],
                                          t_ln_done) if dt == 0 else (),
                                    inc=(dt == NDT - 1)))
                            tkw = []
                            for dt in range(NDT):
                                tkw.append(T(lambda e, dt=dt, hf=hf, wk_=wk_, w13o=w13o:
                                    e.matmul(bank(wk_),
                                             big[:, w13o + 2048 + dt * 128:
                                                 w13o + 2048 + (dt + 1) * 128],
                                             big[:, dt * 1024 + hf * 512:
                                                 dt * 1024 + hf * 512 + 512],
                                             start=(dt == 0), stop=(dt == NDT - 1)),
                                    w=_mx(tk_bank[wk_],) if dt == 0 else (),
                                    inc=(dt == NDT - 1)))
                            tk_bankw[ubk] = tku[-1]
                            tk_bankw[wk_] = tkw[-1]
                            ho = 16384 + jt * 1024 + hf * 512
                            if routed:
                                tt1 = V(lambda e, ubk=ubk, hf=hf: e.tensor_tensor(
                                    qkvr[:, 0:512], bank(ubk),
                                    qkvr[:, 3072 + hf * 512: 3072 + (hf + 1) * 512],
                                    ALU.mult),
                                    w=_mx(tku[-1], tGB, tk_t123_free), inc=True)
                                ts1 = SC(lambda e: e.activation(
                                    qkvr[:, 1024:1536], qkvr[:, 0:512], AF.Silu),
                                    w=(tt1,), inc=True)
                                tt3 = V(lambda e, wk_=wk_, hf=hf: e.tensor_tensor(
                                    qkvr[:, 2048:2560], bank(wk_),
                                    qkvr[:, 3072 + hf * 512: 3072 + (hf + 1) * 512],
                                    ALU.mult), w=_mx(tkw[-1], tGB), inc=True)
                                th = V(lambda e, ho=ho: e.tensor_tensor(
                                    big[:, ho:ho + 512], qkvr[:, 1024:1536],
                                    qkvr[:, 2048:2560], ALU.mult),
                                    w=_mx(ts1, tt3, tk_h_free), inc=True)
                                tk_bank[ubk] = _mx(tk_bank[ubk], tt1)
                                tk_bank[wk_] = _mx(tk_bank[wk_], tt3)
                                tk_t123_free = th
                            else:
                                ts1 = SC(lambda e, ubk=ubk: e.activation(
                                    qkvr[:, 1024:1536], bank(ubk), AF.Silu),
                                    w=_mx(tku[-1], tk_t123_free), inc=True)
                                th = V(lambda e, ho=ho, wk_=wk_: e.tensor_tensor(
                                    big[:, ho:ho + 512], qkvr[:, 1024:1536],
                                    bank(wk_), ALU.mult),
                                    w=_mx(ts1, tkw[-1], tk_h_free), inc=True)
                                tk_bank[ubk] = _mx(tk_bank[ubk], ts1)
                                tk_bank[wk_] = _mx(tk_bank[wk_], th)
                                tk_t123_free = th
                            t_half = _mx(t_half, th)
                        tk_w13buf[par] = t_half
                        t_h_done = _mx(t_h_done, t_half)
                    for tt in range(8):
                        bks = [0, 1, 2, 3] if tt % 2 == 0 else [4, 5, 6, 7]
                        first_w = _mx(t_h_done, tw2, *[tk_bank[bk] for bk in bks])
                        tks_n = {n: [] for n in range(4)}
                        for jt in range(NJT):
                            for nn in range(4):
                                bk = bks[nn]
                                tks_n[nn].append(T(
                                    lambda e, jt=jt, tt=tt, nn=nn, bk=bk: e.matmul(
                                        bank(bk),
                                        big[:, 16384 + jt * 1024 + tt * 128:
                                            16384 + jt * 1024 + tt * 128 + 128],
                                        zreg[:, 16384 + jt * 2048 + nn * 512:
                                             16384 + jt * 2048 + (nn + 1) * 512],
                                        start=(jt == 0), stop=(jt == NJT - 1)),
                                    w=first_w if (jt == 0 and nn == 0) else (),
                                    inc=(jt == NJT - 1)))
                        evs = []
                        for nn in range(4):
                            bk = bks[nn]
                            tk_bankw[bk] = tks_n[nn][-1]
                            ao = tt * 2048 + nn * 512
                            if eidx == 0:
                                tev = V(lambda e, ao=ao, bk=bk: e.tensor_copy(
                                    zreg[:, ao:ao + 512], bank(bk)),
                                    w=_mx(tks_n[nn][-1], t_acc_last[tt]), inc=True)
                            else:
                                tev = V(lambda e, ao=ao, bk=bk: e.tensor_tensor(
                                    zreg[:, ao:ao + 512], bank(bk),
                                    zreg[:, ao:ao + 512], ALU.add),
                                    w=_mx(tks_n[nn][-1], t_acc_last[tt]), inc=True)
                            tk_bank[bk] = tev
                            evs.append(tev)
                        t_acc_last[tt] = _mx(*evs)
                    tk_h_free = _mx(*[t_acc_last[tt] for tt in range(8)])
                    tk_w2buf = tk_h_free
                    tk_gbc_free = t_h_done

                if dbg_mode == "hacc" and c == 0:
                    DMA("sync", dbg_out[:, 0:8192], big[:, 16384:24576],
                        waits=_mx(*[t_acc_last[tt] for tt in range(8)]))
                    DMA("sync", dbg_out[:, 8192:24576],
                        zreg[:, 0:16384],
                        waits=_mx(*[t_acc_last[tt] for tt in range(8)]))
                for tt in range(8):
                    ti = c * 8 + tt
                    bseq = ti // NT
                    trow_i = ti % NT
                    if tt % 2 == 0:
                        xbuf, xfree = xst, tk_x1a_free
                    else:
                        xbuf, xfree = pmpi, tk_x1b_free
                    wxt = DMA("sync", xbuf[:], x1s[ti],
                              waits=_mx(xfree, barrier_A, spill_final),
                              sem="dX" if tt % 2 == 0 else "dX2")
                    if (ne_run > 0 or os.environ.get("MK_ELIST")) and not f1_only:
                        tadd = V(lambda e, xbuf=xbuf, tt=tt: e.tensor_tensor(
                            xbuf[:], xbuf[:], zreg[:, tt * 2048:(tt + 1) * 2048],
                            ALU.add), w=_mx(wxt, t_acc_last[tt]), inc=True)
                    else:
                        tadd = V(lambda e, xbuf=xbuf: e.tensor_copy(
                            xbuf[:, 0:4], xbuf[:, 0:4]),
                            w=_mx(wxt,), inc=True)
                    tout = DMA("sync",
                               y_out[bseq, trow_i * 128:(trow_i + 1) * 128, :],
                               xbuf[:], waits=(tadd,),
                               sem="dOA" if tt % 2 == 0 else "dOB")
                    if tt % 2 == 0:
                        tk_x1a_free = tout
                    else:
                        tk_x1b_free = tout
                t_acc_prev = {tt: t_acc_last[tt] for tt in range(8)}
                tk_gts_free = _mx(t_gdma_prev)

        final = tuple((s, sch.count[s]) for s in DMAS if sch.count[s] > 0)
        sch.op("sync", lambda e: e.nop(), waits=final)

        sch.check_deadlock()

        @blk.sync
        def _(eng):
            sch.replay("sync", eng, sems)

        @blk.gpsimd
        def _(eng):
            sch.replay("gpsimd", eng, sems)

        @blk.tensor
        def _(eng):
            sch.replay("tensor", eng, sems)

        @blk.vector
        def _(eng):
            sch.replay("vector", eng, sems)

        @blk.scalar
        def _(eng):
            sch.replay("scalar", eng, sems)

    return nc


def _prep_inputs(inputs):
    f32 = np.float32
    WQ = np.asarray(inputs["L0_attn_W_Q"], f32)
    WK = np.asarray(inputs["L0_attn_W_K"], f32)
    WV = np.asarray(inputs["L0_attn_W_V"], f32)
    WO = np.asarray(inputs["L0_attn_W_O"], f32)

    def qkv_tile(W):
        a = W.transpose(1, 0, 2)
        a = a.reshape(NDT, 128, H, DH)
        a = a.transpose(2, 1, 0, 3)
        return np.ascontiguousarray(a)

    wq = qkv_tile(WQ)
    wk = qkv_tile(WK)
    wv = qkv_tile(WV)
    wo = np.ascontiguousarray(WO.reshape(H, 128, D))

    bqkv = np.zeros((128, 48), f32)
    bqkv[:, 0:16] = np.asarray(inputs["L0_attn_b_Q"], f32).T
    bqkv[:, 16:32] = np.asarray(inputs["L0_attn_b_K"], f32).T
    bqkv[:, 32:48] = np.asarray(inputs["L0_attn_b_V"], f32).T
    bo = np.ascontiguousarray(np.asarray(inputs["L0_attn_b_O"], f32).reshape(1, D))

    gwm = np.asarray(inputs["L0_ffn_gate_w"], f32)
    gw = np.ascontiguousarray(gwm.T.reshape(NDT, 128, E).transpose(1, 0, 2)
                              .reshape(128, NDT * E))
    eb = np.ascontiguousarray(np.asarray(inputs["L0_ffn_expert_bias"], f32)
                              .reshape(1, E))

    w1 = np.asarray(inputs["L0_ffn_w1"], f32)
    w3 = np.asarray(inputs["L0_ffn_w3"], f32)
    w2m = np.asarray(inputs["L0_ffn_w2"], f32)
    sw1 = np.asarray(inputs["L0_ffn_sw1"], f32)
    sw2 = np.asarray(inputs["L0_ffn_sw2"], f32)
    sw3 = np.asarray(inputs["L0_ffn_sw3"], f32)

    def w13_tile(w):
        a = w.T.reshape(NDT, 128, NJT, 128)
        return a.transpose(2, 1, 0, 3)

    w1b = np.stack([w13_tile(w1[e]) for e in range(E)] + [w13_tile(sw1)])
    w3b = np.stack([w13_tile(w3[e]) for e in range(E)] + [w13_tile(sw3)])
    w13 = np.ascontiguousarray(
        np.stack([w1b, w3b], axis=3).reshape(NE, NJT, 128, 4096))

    def w2_tile(w):
        a = w.T.reshape(NJT, 128, D)
        return a.transpose(1, 0, 2)

    w2 = np.ascontiguousarray(
        np.stack([w2_tile(w2m[e]) for e in range(E)] + [w2_tile(sw2)])
        .reshape(NE, 128, NJT * D))

    x = np.ascontiguousarray(np.asarray(inputs["L0_attn_x"], f32))
    shared = dict(wq=wq, wk=wk, wv=wv, wo=wo, bqkv=bqkv, bo=bo, gw=gw, eb=eb,
                  w13=w13, w2=w2)
    return x, shared


_NC_CACHE = {}


def kernel(**inputs):
    stage = int(os.environ.get("MK_STAGE", "3"))
    stage_key = (stage, os.environ.get("MK_NE", ""), os.environ.get("MK_F1ONLY", ""),
                 os.environ.get("MK_F1LVL", ""), os.environ.get("MK_ELIST", ""),
                 os.environ.get("MK_DBG", ""))
    x, shared = _prep_inputs(inputs)
    if stage_key not in _NC_CACHE:
        _NC_CACHE[stage_key] = build_program(stage=stage)
    nc = _NC_CACHE[stage_key]
    in_maps = []
    for c in range(NCORES):
        m = dict(shared)
        m["x"] = np.ascontiguousarray(x[c * BPC:(c + 1) * BPC])
        in_maps.append(m)
    res = run_bass_kernel_spmd(nc, in_maps, core_ids=list(range(NCORES)))
    if os.environ.get("MK_DBG"):
        kernel.dbg = [r["dbg"] for r in res.results]
    return np.concatenate([r["y"] for r in res.results], axis=0)


if __name__ == "__main__":
    import time
    t0 = time.time()
    nc = build_program(stage=int(os.environ.get("MK_STAGE", "3")))
    print(f"built ok in {time.time() - t0:.1f}s")


# version 09
# speedup vs baseline: 1.0126x; 1.0126x over previous
import os
from contextlib import ExitStack

import numpy as np
import ml_dtypes
import concourse.bass as bass
import concourse.mybir as mybir
from concourse.bass_utils import run_bass_kernel_spmd

F32 = mybir.dt.float32
BF16 = mybir.dt.bfloat16
AF = mybir.ActivationFunctionType
ALU = mybir.AluOpType
AX = mybir.AxisListType

BATCH, S, D, H, DH, E, HID = 16, 2048, 2048, 16, 128, 8, 1024
NCORES = 8
BPC = BATCH // NCORES
NT = S // 128
NDT = D // 128
NJT = HID // 128
TC = 1024
NCHUNK = (BPC * S) // TC
NE = E + 1
ATTN_SCALE = np.float32(np.sqrt(np.float32(DH)))
CINV = float(np.float32(1.0) / ATTN_SCALE)
SQRT_DH = float(ATTN_SCALE)
LN_EPS = 1e-5
MASK_NEG = -2.0e6

ENG = ("sync", "gpsimd", "tensor", "vector", "scalar")
DMAS = ("dS", "dC", "dR", "dW", "dX", "dX2", "dPI0", "dPI1", "dPO0", "dPO1",
        "dSP", "dF0", "dF1", "dW2", "dOA", "dOB")
SEMS = DMAS + ("tS", "vS", "sS", "gS")
ESEM = {"tensor": "tS", "vector": "vS", "scalar": "sS", "gpsimd": "gS"}


class Sched:
    def __init__(self):
        self.ops = []
        self.count = {s: 0 for s in SEMS}

    def dma(self, queue, dst, src, waits=(), sem="dS"):
        self.count[sem] += 16
        tok = (sem, self.count[sem])
        self.ops.append((queue, _mx(waits),
                         (lambda eng, dst=dst, src=src: eng.dma_start(dst, src)),
                         sem, 16))
        return tok

    def op(self, engine, emit_fn, waits=(), inc=False):
        sem = ESEM[engine] if inc else None
        tok = None
        if sem:
            self.count[sem] += 1
            tok = (sem, self.count[sem])
        self.ops.append((engine, _mx(waits), emit_fn, sem, 1 if sem else 0))
        return tok

    def barrier_token(self):
        return tuple((s, v) for s, v in self.count.items() if v > 0)

    def check_deadlock(self):
        qs = {e: [] for e in ENG}
        for i, (eng, waits, _f, sem, amt) in enumerate(self.ops):
            qs[eng].append((i, waits, sem, amt))
        heads = {e: 0 for e in ENG}
        sem_val = {s: 0 for s in SEMS}
        progress = True
        while progress:
            progress = False
            for e in ENG:
                while heads[e] < len(qs[e]):
                    _i, waits, sem, amt = qs[e][heads[e]]
                    if all(sem_val[s] >= v for s, v in waits):
                        if sem:
                            sem_val[sem] += amt
                        heads[e] += 1
                        progress = True
                    else:
                        break
        stuck = {e: len(qs[e]) - heads[e] for e in ENG if heads[e] < len(qs[e])}
        if stuck:
            detail = []
            for e in ENG:
                if heads[e] < len(qs[e]):
                    i, waits, _s, _a = qs[e][heads[e]]
                    unmet = [(s, v, sem_val[s]) for s, v in waits if sem_val[s] < v]
                    detail.append(f"{e}@op{i} needs {unmet}")
            raise RuntimeError(f"schedule deadlock: {stuck}; " + "; ".join(detail))

    def replay(self, name, eng, sems):
        for engine, waits, emit_fn, sem, amt in self.ops:
            if engine != name:
                continue
            for s, v in waits:
                eng.wait_ge(sems[s], v)
            inst = emit_fn(eng)
            if sem:
                inst.then_inc(sems[sem], amt)


def _mx(*tokens):
    best = {}
    for t in tokens:
        if not t:
            continue
        if isinstance(t[0], str):
            t = (t,)
        for s, v in t:
            if v > best.get(s, -1):
                best[s] = v
    return tuple(best.items())


def build_program(stage=3):
    nc = bass.Bass("TRN2", target_bir_lowering=False, debug=False)

    x_in = nc.dram_tensor("x", [BPC, S, D], F32, kind="ExternalInput").ap()
    wq_in = nc.dram_tensor("wq", [H, 128, NDT, DH], F32, kind="ExternalInput").ap()
    wk_in = nc.dram_tensor("wk", [H, 128, NDT, DH], F32, kind="ExternalInput").ap()
    wv_in = nc.dram_tensor("wv", [H, 128, NDT, DH], F32, kind="ExternalInput").ap()
    wo_in = nc.dram_tensor("wo", [H, 128, D], F32, kind="ExternalInput").ap()
    bqkv_in = nc.dram_tensor("bqkv", [128, 48], F32, kind="ExternalInput").ap()
    bo_in = nc.dram_tensor("bo", [1, D], F32, kind="ExternalInput").ap()
    gw_in = nc.dram_tensor("gw", [128, NDT * E], F32, kind="ExternalInput").ap()
    eb_in = nc.dram_tensor("eb", [1, E], F32, kind="ExternalInput").ap()
    w13_in = nc.dram_tensor("w13", [NE, NJT, 128, 4096], F32, kind="ExternalInput").ap()
    w2_in = nc.dram_tensor("w2", [NE, 128, NJT * D], F32, kind="ExternalInput").ap()
    y_out = nc.dram_tensor("y", [BPC, S, D], F32, kind="ExternalOutput").ap()
    dbg_mode = os.environ.get("MK_DBG", "")
    dbg_out = (nc.dram_tensor("dbg", [128, 32768], BF16, kind="ExternalOutput").ap()
               if dbg_mode else None)

    w13s = nc.dram_tensor("w13s", [NE, NJT, 128, 4096], BF16, kind="Internal").ap()
    w2s = nc.dram_tensor("w2s", [NE, 128, NJT * D], BF16, kind="Internal").ap()
    x1s = nc.dram_tensor("x1s", [BPC * NT, 128, D], F32, kind="Internal").ap()

    ident_bf = nc.inline_tensor(np.eye(128, dtype=ml_dtypes.bfloat16), name="idbf").ap()
    ident_f = nc.inline_tensor(np.eye(128, dtype=np.float32), name="idf").ap()
    tril = np.tril(np.ones((128, 128), bool))
    minfl = nc.inline_tensor(
        np.where(tril, np.float32(3.0e38), np.float32(MASK_NEG)).astype(np.float32),
        name="minfl").ap()
    minfu = nc.inline_tensor(
        np.where(tril.T, np.float32(3.0e38), np.float32(MASK_NEG)).astype(np.float32),
        name="minfu").ap()
    ones1 = nc.inline_tensor(np.ones((1, 128), np.float32), name="ones1").ap()
    rev_c = nc.inline_tensor(
        np.tile(np.arange(E, 0, -1, dtype=np.float32)[None, :], (128, 1)),
        name="revc").ap()
    ones2_c = nc.inline_tensor(np.ones((2, 128), ml_dtypes.bfloat16), name="ones2").ap()
    sel8_np = np.zeros((8, 8, 128), ml_dtypes.bfloat16)
    for _e in range(8):
        sel8_np[_e, _e, :] = 1.0
    sel8_c = nc.inline_tensor(np.ascontiguousarray(
        sel8_np.transpose(1, 0, 2).reshape(8, 1024)), name="sel8").ap()
    eps_c = nc.inline_tensor(np.full((128, 1), LN_EPS, np.float32), name="epsc").ap()

    sch = Sched()
    stk = ExitStack()
    sb = lambda name, shape, dt: stk.enter_context(nc.sbuf_tensor(name, shape, dt))

    with stk:
        big = sb("big", [128, 32768], BF16)
        zreg = sb("zreg", [128, 32768], BF16)
        qkvr = sb("qkvr", [128, 4096], F32)
        fstage = sb("fstage", [128, 2048], F32)
        wbf = sb("wbf", [128, 4096], BF16)
        xst = sb("xst", [128, 2048], F32)
        pmpi = sb("pmpi", [128, 2048], F32)
        pmpo = sb("pmpo", [128, 2048], BF16)
        vtm2 = sb("vtm2", [128, 2048], BF16)
        mbc = sb("mbc", [128, 1024], F32)
        ebuf = sb("ebuf", [128, 1024], BF16)
        junkb = sb("junkb", [128, 512], BF16)
        stats = sb("stats", [128, 64], F32)
        row1 = sb("row1", [1, 1024], F32)
        gts = sb("gts", [8, TC], BF16)
        sel8_s = sb("sel8_s", [8, 1024], BF16)
        ones2_s = sb("ones2_s", [2, 128], BF16)
        g8 = sb("g8", [128, 64], F32)
        idb_s = sb("idb_s", [128, 128], BF16)
        idf_s = sb("idf_s", [128, 128], F32)
        mfl_s = sb("mfl_s", [128, 128], F32)
        mfu_s = sb("mfu_s", [128, 128], F32)
        ones_s = sb("ones_s", [1, 128], F32)
        bqkv_s = sb("bqkv_s", [128, 48], F32)
        bo_bf = sb("bo_bf", [128, 2048], BF16)
        gw_s = sb("gw_s", [128, 128], F32)
        eb_s = sb("eb_s", [128, 8], F32)
        rev_s = sb("rev_s", [128, 8], F32)
        eps_s = sb("eps_s", [128, 1], F32)
        arena = stk.enter_context(nc.psum_tensor("arena", [128, 4096], F32))

        sems = {s: stk.enter_context(nc.semaphore(s)) for s in SEMS}
        blk = stk.enter_context(nc.Block())

        arena_ap = arena[:]
        arena_bf = arena_ap.bitcast(BF16)

        def bank(i, w=512, off=0):
            return arena_ap[:, i * 512 + off: i * 512 + off + w]

        def bank_bf(i, w=1024, off=0):
            return arena_bf[:, i * 1024 + off: i * 1024 + off + w]

        qkv_bf = qkvr[:].bitcast(BF16)
        mbc_bf = mbc[:].bitcast(BF16)
        def echunk(epar, wdt=512):
            if epar < 2:
                return ebuf[:, epar * 512: epar * 512 + wdt]
            return mbc_bf[:, (epar - 2) * 512: (epar - 2) * 512 + wdt]
        qh = qkv_bf[:, 0:2048]
        kh = qkv_bf[:, 2048:4096]
        vfm = qkv_bf[:, 4096:6144]
        vtm = qkv_bf[:, 6144:8192]
        xst_bf = None
        big_a16 = big[:].rearrange("p (a b) -> p a b", a=16)
        big_a32 = big[:].rearrange("p (a b) -> p a b", a=32)

        T = lambda f, w=(), inc=False: sch.op("tensor", f, w, inc)
        V = lambda f, w=(), inc=False: sch.op("vector", f, w, inc)
        SC = lambda f, w=(), inc=False: sch.op("scalar", f, w, inc)
        DMA = sch.dma

        for _dst, _src in ((idb_s[:], ident_bf), (idf_s[:], ident_f),
                           (mfl_s[:], minfl), (mfu_s[:], minfu),
                           (ones_s[:], ones1), (bqkv_s[:], bqkv_in),
                           (rev_s[:], rev_c), (gw_s[:], gw_in),
                           (eps_s[:], eps_c), (sel8_s[:], sel8_c),
                           (ones2_s[:], ones2_c)):
            DMA("sync", _dst, _src, sem="dC")
        tCN = (("dC", sch.count["dC"]),)

        tk_bo = []
        prev_cp = ()
        for hh in range(2):
            tdr = DMA("sync", row1[0:1, :], bo_in[0:1, hh * 1024:(hh + 1) * 1024],
                      waits=_mx(prev_cp), sem="dR")
            for ch in range(2):
                tmm = T(lambda e, hh=hh, ch=ch: e.matmul(
                    bank(6), ones_s[0:1, :], row1[0:1, ch * 512:(ch + 1) * 512],
                    start=True, stop=True), w=_mx(tdr, prev_cp, tCN), inc=True)
                prev_cp = V(lambda e, hh=hh, ch=ch: e.tensor_copy(
                    bo_bf[:, hh * 1024 + ch * 512: hh * 1024 + (ch + 1) * 512],
                    bank(6)), w=(tmm,), inc=True)
                tk_bo.append(prev_cp)
        tdr = DMA("sync", row1[0:1, 0:8], eb_in, waits=_mx(prev_cp), sem="dR")
        tmm = T(lambda e: e.matmul(bank(6, 8), ones_s[0:1, :], row1[0:1, 0:8],
                                   start=True, stop=True),
                w=_mx(tdr, prev_cp), inc=True)
        tEB = V(lambda e: e.tensor_copy(eb_s[:, 0:8], bank(6, 8)), w=(tmm,), inc=True)
        tBO = _mx(*tk_bo)
        tk_row1_free = tEB
        tk_bank = {i: () for i in range(8)}
        tk_bank[6] = _mx(tEB, tBO)

        pump_steps = []
        for ee in range(NE):
            for jt in range(NJT):
                for qq in range(4):
                    pump_steps.append((w13_in[ee, jt, :, qq * 1024:(qq + 1) * 1024],
                                       w13s[ee, jt, :, qq * 1024:(qq + 1) * 1024],
                                       ("w13", ee, jt)))
            for qq in range(16):
                pump_steps.append((w2_in[ee, :, qq * 1024:(qq + 1) * 1024],
                                   w2s[ee, :, qq * 1024:(qq + 1) * 1024],
                                   ("w2", ee)))
        pump_tok = {}
        pstate = {"i": 0, "in": [(), ()], "cast": [(), ()], "out": [(), ()]}

        def pump_issue_in(i):
            if i >= len(pump_steps):
                return
            src = pump_steps[i][0]
            par = i % 2
            pstate["in"][par] = DMA("sync", pmpi[:, par * 1024:(par + 1) * 1024],
                                    src, waits=_mx(pstate["cast"][par]),
                                    sem="dPI%d" % par)

        def pump_tick():
            i = pstate["i"]
            if i >= len(pump_steps):
                return
            pstate["i"] = i + 1
            if i == 0:
                pump_issue_in(0)
                pump_issue_in(1)
            par = i % 2
            dst, key = pump_steps[i][1], pump_steps[i][2]
            tcast = SC(lambda e, par=par: e.activation(
                pmpo[:, par * 1024:(par + 1) * 1024],
                pmpi[:, par * 1024:(par + 1) * 1024], AF.Copy),
                w=_mx(pstate["in"][par], pstate["out"][par]), inc=True)
            pstate["cast"][par] = tcast
            tout = DMA("sync", dst, pmpo[:, par * 1024:(par + 1) * 1024],
                       waits=(tcast,), sem="dPO%d" % par)
            pstate["out"][par] = tout
            pump_tok[key] = tout
            pump_issue_in(i + 2)

        def pump_flush():
            while pstate["i"] < len(pump_steps):
                pump_tick()

        def emit_ln(xsrc, wx, par, junk_guard=(), tm_guard=(), want_tm=True):
            c0, c1 = 24 + 3 * par, 25 + 3 * par
            tsum = V(lambda e: e.reduce_sum(stats[:, c0:c0 + 1], xsrc, axis=AX.X),
                     w=wx, inc=True)
            tneg = V(lambda e: e.tensor_scalar_mul(stats[:, c0:c0 + 1],
                                                   stats[:, c0:c0 + 1], -1.0 / D),
                     w=(tsum,), inc=True)
            tsq = SC(lambda e: e.activation(wbf[:, 0:2048], xsrc, AF.Square,
                                            bias=stats[:, c0:c0 + 1],
                                            accum_out=stats[:, c1:c1 + 1]),
                     w=_mx(tneg, wx, junk_guard), inc=True)
            tsd = SC(lambda e: e.activation(stats[:, c1:c1 + 1], stats[:, c1:c1 + 1],
                                            AF.Sqrt, bias=eps_s[:, 0:1], scale=1.0 / D),
                     w=(tsq,), inc=True)
            trs = V(lambda e: e.reciprocal(stats[:, c1:c1 + 1], stats[:, c1:c1 + 1]),
                    w=(tsd,), inc=True)
            ttm = None
            if want_tm:
                ttm = V(lambda e: e.tensor_scalar(wbf[:, 2048:4096], xsrc,
                                                  stats[:, c0:c0 + 1],
                                                  stats[:, c1:c1 + 1],
                                                  ALU.add, ALU.mult),
                        w=_mx(trs, tm_guard), inc=True)
            return {"sum": tsum, "neg": tneg, "sq": tsq, "sd": tsd, "rs": trs,
                    "tm": ttm, "c0": c0, "c1": c1}

        tk_bankw = {i: () for i in range(8)}
        tk_wbf_free = [(), ()]
        tk_fstage_free = ()
        tk_xst_free = ()
        tk_qkv_free = {"q": (), "k": (), "vf": (), "vt": ()}
        tk_ebuf = [(), (), (), ()]
        tk_stats_par = [(), ()]
        tk_mbc_free = ()
        tk_junk_guard = ()
        tk_tm_guard = ()
        tk_x1 = {}
        tk_z_done = {}

        tk_parity_free = {0: (), 1: ()}
        for b in range(BPC):
            tk_junk_guard = _mx(tk_junk_guard, tk_wbf_free[0])
            tk_tm_guard = _mx(tk_tm_guard, tk_wbf_free[1])
            tk_xln = ()
            for tt in range(NT):
                wx = _mx(DMA("sync", xst[:], x_in[b, tt * 128:(tt + 1) * 128, :],
                             waits=tk_xst_free, sem="dX"))
                ln = emit_ln(xst[:], wx, tt % 2, junk_guard=tk_junk_guard,
                             tm_guard=tk_tm_guard)
                tk_junk_guard = ln["sq"]
                tk_xst_free = ln["tm"]
                tr_last = ()
                for r in range(2):
                    bk = 6 + r
                    tks = []
                    for i in range(8):
                        dt = r * 8 + i
                        tks.append(T(lambda e, dt=dt, i=i, bk=bk: e.transpose(
                            bank_bf(bk, 128, i * 128),
                            wbf[:, 2048 + dt * 128: 2048 + (dt + 1) * 128],
                            idb_s[:]),
                            w=_mx(ln["tm"], tk_bank[bk], tCN) if i == 0 else (),
                            inc=(i == 7)))
                    tk_bankw[bk] = tks[-1]
                    tev = V(lambda e, r=r, tt=tt, bk=bk: e.tensor_copy(
                        big_a16[:, r * 8:(r + 1) * 8, tt * 128:tt * 128 + 128],
                        bank_bf(bk).rearrange("p (a b) -> p a b", a=8)),
                        w=(tks[-1],), inc=True)
                    tk_bank[bk] = tev
                    tk_xln = _mx(tk_xln, tev)
                    tr_last = _mx(tr_last, tks[-1])
                tk_tm_guard = tr_last
                pump_tick()

            qk2 = xst[:].bitcast(BF16)
            hstate = {}
            tk_g8_free = {0: (), 1: ()}
            sc_rot = {"i": 0}
            tk_xln_reads = ()
            def next_bk():
                sc_rot["i"] = (sc_rot["i"] + 1) % 4
                return sc_rot["i"]
            tk_xln_cur = tk_xln

            def gen_fills(h):
                nonlocal tk_fstage_free, tk_xln_reads
                par = h % 2
                if par == 0:
                    bq_, bk_, bvt = qh, kh, vtm
                else:
                    bq_, bk_, bvt = qk2[:, 0:2048], qk2[:, 2048:4096], vtm2[:]
                free_tok = tk_parity_free[par]
                projs = (("q", wq_in, 0, bq_, "bq"), ("k", wk_in, 1, bk_, "bk"),
                         ("vf", wv_in, 0, vfm, None))
                tk_fill = {}
                for nm, w_mat, slot, dstbuf, biasnm in projs:
                    wdma = DMA("sync", fstage[:],
                               w_mat[h].rearrange("p a b -> p (a b)"),
                               waits=_mx(tk_fstage_free), sem="dW")
                    guard = tk_junk_guard if slot == 0 else tk_tm_guard
                    tcast = SC(lambda e, slot=slot: e.activation(
                        wbf[:, slot * 2048:(slot + 1) * 2048], fstage[:], AF.Copy),
                        w=_mx(wdma, tk_wbf_free[slot], guard), inc=True)
                    tk_fstage_free = tcast
                    buf_guard = free_tok if nm in ("q", "k") else tk_qkv_free["vf"]
                    fill_last = ()
                    fill_mm_last = ()
                    for chn in range(4):
                        bk = 6 + chn % 2
                        wfirst = _mx(tcast, tk_bank[bk], tk_xln_cur, buf_guard)
                        tks = []
                        for dt in range(NDT):
                            tks.append(T(lambda e, dt=dt, chn=chn, slot=slot, bk=bk:
                                e.matmul(bank(bk),
                                         wbf[:, slot * 2048 + dt * 128:
                                             slot * 2048 + (dt + 1) * 128],
                                         big[:, dt * 2048 + chn * 512:
                                             dt * 2048 + chn * 512 + 512],
                                         start=(dt == 0), stop=(dt == NDT - 1)),
                                w=wfirst if dt == 0 else (), inc=(dt == NDT - 1)))
                        tk_bankw[bk] = tks[-1]
                        fill_mm_last = _mx(fill_mm_last, tks[-1])
                        if biasnm is not None:
                            bcol = {"bq": 0, "bk": 16}[biasnm] + h
                            tev = V(lambda e, chn=chn, dstbuf=dstbuf, bcol=bcol, bk=bk:
                                e.tensor_scalar_add(
                                    dstbuf[:, chn * 512:(chn + 1) * 512],
                                    bank(bk), bqkv_s[:, bcol:bcol + 1]),
                                w=(tks[-1],), inc=True)
                        else:
                            tev = V(lambda e, chn=chn, dstbuf=dstbuf, bk=bk:
                                e.tensor_copy(dstbuf[:, chn * 512:(chn + 1) * 512],
                                              bank(bk)),
                                w=(tks[-1],), inc=True)
                        tk_bank[bk] = _mx(tk_bank[bk], tev)
                        fill_last = _mx(fill_last, tev)
                        yield
                    tk_fill[nm] = fill_last
                    tk_wbf_free[slot] = fill_mm_last
                    pump_tick()
                tvt = ()
                for r in range(2):
                    bk = 6 + r
                    tks = []
                    for i in range(8):
                        tt2 = r * 8 + i
                        tks.append(T(lambda e, tt2=tt2, i=i, bk=bk: e.transpose(
                            bank_bf(bk, 128, i * 128),
                            vfm[:, tt2 * 128:(tt2 + 1) * 128], idb_s[:]),
                            w=_mx(tk_fill["vf"], tk_bank[bk],
                                  free_tok) if i == 0 else (),
                            inc=(i == 7)))
                    tk_bankw[bk] = tks[-1]
                    tev = V(lambda e, r=r, bk=bk, bvt=bvt: e.tensor_copy(
                        bvt[:, r * 1024:(r + 1) * 1024], bank_bf(bk)),
                        w=(tks[-1],), inc=True)
                    tk_bank[bk] = tev
                    tvt = _mx(tvt, tev)
                    yield
                tk_qkv_free["vf"] = tvt
                tk_xln_reads = _mx(tk_xln_reads, tk_fill["q"], tk_fill["k"],
                                   tk_fill["vf"])
                hstate[h] = dict(fq=tk_fill["q"], fk=tk_fill["k"], tvt=tvt,
                                 q=bq_, k=bk_, vt=bvt)

            def gen_pass1(h):
                st = hstate[h]
                bq_, bk_ = st["q"], st["k"]
                goff = 16 * (h % 2)
                t_m2c = []
                for qt in range(NT):
                    par = qt % 2
                    so = 32 * par
                    kend = (qt + 1) * 128
                    nch = (kend + 511) // 512
                    t_sum_toks = []
                    t_m_toks = []
                    guard = tk_stats_par[par]
                    for ch in range(nch):
                        lo = ch * 512
                        wdt = min(512, kend - lo)
                        bk = next_bk()
                        tmm = T(lambda e, qt=qt, lo=lo, wdt=wdt, bk=bk, bq_=bq_, bk_=bk_:
                            e.matmul(bank(bk, wdt), bq_[:, qt * 128:(qt + 1) * 128],
                                     bk_[:, lo:lo + wdt], start=True, stop=True),
                            w=_mx(st["fq"], st["fk"], tk_bank[bk]), inc=True)
                        tk_bankw[bk] = tmm
                        tpre = tmm
                        if lo + wdt == kend:
                            tpre = V(lambda e, bk=bk, wdt=wdt: e.tensor_tensor(
                                bank(bk, 128, wdt - 128), bank(bk, 128, wdt - 128),
                                mfl_s[:], ALU.min), w=_mx(tmm, tCN), inc=True)
                        tmax = V(lambda e, bk=bk, wdt=wdt, so=so, ch=ch: e.reduce_max(
                            stats[:, so + ch:so + ch + 1], bank(bk, wdt), axis=AX.X),
                            w=_mx(tpre, guard), inc=True)
                        tngm = SC(lambda e, so=so, ch=ch: e.mul(
                            stats[:, so + 10 + ch:so + 11 + ch],
                            stats[:, so + ch:so + ch + 1], -CINV),
                            w=_mx(tmax, guard), inc=True)
                        texp = SC(lambda e, bk=bk, wdt=wdt, so=so, ch=ch: e.activation(
                            junkb[:, 0:wdt], bank(bk, wdt), AF.Exp,
                            bias=stats[:, so + 10 + ch:so + 11 + ch], scale=CINV,
                            accum_out=stats[:, so + 4 + ch:so + 5 + ch]),
                            w=(tngm,), inc=True)
                        tk_bank[bk] = _mx(tk_bank[bk], texp)
                        t_sum_toks.append(texp)
                        t_m_toks.append(tmax)
                        yield
                    tm = V(lambda e, so=so, nch=nch: e.reduce_max(
                        stats[:, so + 8:so + 9], stats[:, so:so + nch], axis=AX.X),
                        w=_mx(*t_m_toks), inc=True)
                    tnm = SC(lambda e, so=so: e.mul(
                        stats[:, so + 9:so + 10], stats[:, so + 8:so + 9], -CINV),
                        w=(tm,), inc=True)
                    tec = SC(lambda e, so=so, nch=nch: e.activation(
                        stats[:, so + 14:so + 14 + nch], stats[:, so:so + nch],
                        AF.Exp, bias=stats[:, so + 9:so + 10], scale=CINV),
                        w=_mx(tnm, *t_sum_toks), inc=True)
                    tpr = V(lambda e, so=so, nch=nch: e.tensor_tensor(
                        stats[:, so + 18:so + 18 + nch],
                        stats[:, so + 4:so + 4 + nch],
                        stats[:, so + 14:so + 14 + nch], ALU.mult),
                        w=(tec,), inc=True)
                    tsm = V(lambda e, so=so, nch=nch: e.reduce_sum(
                        stats[:, so + 22:so + 23], stats[:, so + 18:so + 18 + nch],
                        axis=AX.X), w=(tpr,), inc=True)
                    tln = SC(lambda e, so=so: e.activation(
                        stats[:, so + 23:so + 24], stats[:, so + 22:so + 23], AF.Ln),
                        w=(tsm,), inc=True)
                    tmc = V(lambda e, so=so, qt=qt, goff=goff: e.tensor_scalar(
                        g8[:, goff + qt:goff + qt + 1], stats[:, so + 23:so + 24],
                        SQRT_DH, stats[:, so + 8:so + 9], ALU.mult, ALU.add),
                        w=_mx(tln, tk_g8_free[h % 2] if qt < 2 else ()), inc=True)
                    tk_stats_par[par] = _mx(tln, tmc)
                    t_m2c.append(tmc)
                    pump_tick()
                    yield
                hstate[h]["m2c"] = t_m2c


            def gen_pass2(h):
                nonlocal tk_row1_free, tk_mbc_free
                st = hstate[h]
                bq_, bk_, bvt = st["q"], st["k"], st["vt"]
                goff = 16 * (h % 2)
                t_m2c = hstate[h]["m2c"]
                t_z_evac = ()
                ech = 0
                for half in range(2):
                    ttrs = []
                    for j in range(8):
                        qt = half * 8 + j
                        bkj = 2 + j // 4
                        ttrs.append(T(lambda e, j=j, qt=qt, goff=goff: e.transpose(
                            arena_ap[0:1, 1024 + j * 128: 1024 + (j + 1) * 128],
                            g8[:, goff + qt:goff + qt + 1], idf_s[:]),
                            w=_mx(t_m2c[qt], tk_bank[bkj] if j % 4 == 0 else (),
                                  tCN),
                            inc=(j in (3, 7))))
                    tk_bankw[2] = ttrs[3]
                    tk_bankw[3] = ttrs[7]
                    tk_g8_free[h % 2] = _mx(tk_g8_free[h % 2], ttrs[7])
                    thi = V(lambda e: e.tensor_scalar_mul(
                        mbc_bf[0:1, 1024:2048], arena_ap[0:1, 1024:2048], -1.0),
                        w=_mx(ttrs[3], ttrs[7], tk_row1_free), inc=True)
                    tlo = V(lambda e: e.scalar_tensor_tensor(
                        gts[0:1, 0:1024], arena_ap[0:1, 1024:2048], -1.0,
                        mbc_bf[0:1, 1024:2048], ALU.mult, ALU.subtract),
                        w=(thi,), inc=True)
                    tk_bank[2] = _mx(tk_bank[2], tlo)
                    tk_bank[3] = _mx(tk_bank[3], tlo)
                    tdl = DMA("sync", mbc_bf[1:2, 1024:2048], gts[0:1, 0:1024],
                              waits=(tlo,), sem="dR")
                    tMB = _mx(tlo, tdl)
                    tk_row1_free = ()
                    t_mbc_used = ()
                    t_z_last = ()
                    kmax = 8 if half == 0 else 16
                    chunks = []
                    for kt in range(kmax):
                        glo = max(kt * 128, half * 1024)
                        llo = glo - half * 1024
                        cuts = sorted(set([llo] + ([512] if llo < 512 else []) + [1024]))
                        first = True
                        for ci in range(len(cuts) - 1):
                            chunks.append((kt, cuts[ci], cuts[ci + 1] - cuts[ci],
                                           first and glo == kt * 128, kt == 0 and first))
                            first = False

                    pend = None
                    for (kt, lo, wdt, diag, zstart) in chunks:
                        bk = next_bk()
                        epar = ech % 4
                        ech += 1
                        tbias = T(lambda e, lo=lo, wdt=wdt, bk=bk: e.matmul(
                            bank(bk, wdt), ones2_s[0:2, :],
                            mbc_bf[0:2, 1024 + lo: 1024 + lo + wdt],
                            start=True, stop=False),
                            w=_mx(tk_bank[bk], tMB, tCN), inc=False)
                        tmm = T(lambda e, kt=kt, lo=lo, wdt=wdt, half=half, bk=bk,
                                bq_=bq_, bk_=bk_:
                            e.matmul(bank(bk, wdt),
                                     bk_[:, kt * 128:(kt + 1) * 128],
                                     bq_[:, half * 1024 + lo: half * 1024 + lo + wdt],
                                     start=False, stop=True),
                            w=_mx(st["fq"], st["fk"]),
                            inc=True)
                        tk_bankw[bk] = tmm
                        t_mbc_used = _mx(t_mbc_used, tmm)
                        tk_row1_free = _mx(tk_row1_free, tmm)
                        tpre = tmm
                        if diag:
                            tpre = V(lambda e, bk=bk: e.tensor_tensor(
                                bank(bk, 128), bank(bk, 128), mfu_s[:], ALU.min),
                                w=(tmm,), inc=True)
                        texp = SC(lambda e, bk=bk, wdt=wdt, epar=epar:
                            e.activation(echunk(epar, wdt),
                                         bank(bk, wdt), AF.Exp, scale=CINV),
                            w=_mx(tpre, tk_ebuf[epar]), inc=True)
                        tk_bank[bk] = _mx(tk_bank[bk], texp)
                        yield
                        if pend is not None:
                            (pkt, plo, pwdt, pzstart, ptexp, pepar) = pend
                            tzm = T(lambda e, kt=pkt, lo=plo, wdt=pwdt, epar=pepar,
                                    bvt=bvt:
                                e.matmul(arena_ap[:, 2048 + lo: 2048 + lo + wdt],
                                         bvt[:, kt * 128:(kt + 1) * 128],
                                         echunk(epar, wdt),
                                         start=(kt == 0), stop=False),
                                w=_mx(ptexp, st["tvt"],
                                      tk_bank[4] if pzstart else ()),
                                inc=True)
                            tk_ebuf[pepar] = tzm
                            tk_bankw[4] = tzm
                            t_z_last = _mx(t_z_last, tzm)
                        pend = (kt, lo, wdt, zstart, texp, epar)
                        if ech % 4 == 0:
                            pump_tick()
                        yield
                    if pend is not None:
                        (pkt, plo, pwdt, pzstart, ptexp, pepar) = pend
                        tzm = T(lambda e, kt=pkt, lo=plo, wdt=pwdt, epar=pepar,
                                bvt=bvt:
                            e.matmul(arena_ap[:, 2048 + lo: 2048 + lo + wdt],
                                     bvt[:, kt * 128:(kt + 1) * 128],
                                     echunk(epar, wdt),
                                     start=(kt == 0), stop=False),
                            w=_mx(ptexp, st["tvt"],
                                  tk_bank[4] if pzstart else ()),
                            inc=True)
                        tk_ebuf[pepar] = tzm
                        tk_bankw[4] = tzm
                        t_z_last = _mx(t_z_last, tzm)
                        pend = None
                    bvcol = 32 + h
                    tze = V(lambda e, h=h, half=half, bvcol=bvcol: e.tensor_scalar_add(
                        zreg[:, h * 2048 + half * 1024: h * 2048 + (half + 1) * 1024],
                        arena_ap[:, 2048:3072], bqkv_s[:, bvcol:bvcol + 1]),
                        w=_mx(t_z_last,), inc=True)
                    tk_bank[4] = tze
                    tk_bank[5] = tze
                    tk_mbc_free = t_mbc_used
                    t_z_evac = _mx(t_z_evac, tze)
                tk_z_done[(b, h)] = t_z_evac
                tk_parity_free[h % 2] = t_z_evac
                del hstate[h]

            def drive(lanes):
                live = [g for g in lanes if g is not None]
                while live:
                    for g in list(live):
                        try:
                            next(g)
                        except StopIteration:
                            live.remove(g)

            def lane_chain(*gens):
                for g in gens:
                    if g is not None:
                        yield from g

            drive([lane_chain(gen_fills(0), gen_pass1(0))])
            for h in range(H):
                lane2 = (lane_chain(gen_fills(h + 1), gen_pass1(h + 1))
                         if h + 1 < H else None)
                drive([gen_pass2(h), lane2])
            tk_xst_free = _mx(tk_xst_free, tk_parity_free[1])

            tk_allz = _mx(*[tk_z_done[(b, hh)] for hh in range(H)])
            tk_wo = ()
            for ht in range(H):
                wdma = DMA("sync", fstage[:], wo_in[ht], waits=tk_fstage_free,
                           sem="dW")
                tcast = SC(lambda e, ht=ht: e.activation(
                    big[:, ht * 2048:(ht + 1) * 2048], fstage[:], AF.Copy),
                    w=_mx(wdma, tk_xln_reads if ht == 0 else ()), inc=True)
                tk_fstage_free = tcast
                tk_wo = _mx(tk_wo, tcast)
                if ht % 4 == 0:
                    pump_tick()
            for qt in range(NT):
                ti = b * NT + qt
                wxt = DMA("sync", xst[:], x_in[b, qt * 128:(qt + 1) * 128, :],
                          waits=tk_xst_free, sem="dX")
                t_nch = ()
                for chn in range(4):
                    bk = 6 + chn % 2
                    tks = []
                    for ht in range(H):
                        tks.append(T(lambda e, qt=qt, chn=chn, ht=ht, bk=bk:
                            e.matmul(bank(bk),
                                     zreg[:, ht * 2048 + qt * 128:
                                          ht * 2048 + (qt + 1) * 128],
                                     big[:, ht * 2048 + chn * 512:
                                         ht * 2048 + chn * 512 + 512],
                                     start=(ht == 0), stop=(ht == H - 1)),
                            w=_mx(tk_wo, tk_allz, tk_bank[bk]) if ht == 0 else (),
                            inc=(ht == H - 1)))
                    tk_bankw[bk] = tks[-1]
                    tv1 = V(lambda e, chn=chn, bk=bk: e.tensor_tensor(
                        mbc[:, 0:512], bank(bk),
                        bo_bf[:, chn * 512:(chn + 1) * 512], ALU.add),
                        w=_mx(tks[-1], tBO, tk_mbc_free), inc=True)
                    tk_bank[bk] = tv1
                    tk_mbc_free = ()
                    tv2 = V(lambda e, chn=chn: e.tensor_tensor(
                        xst[:, chn * 512:(chn + 1) * 512],
                        xst[:, chn * 512:(chn + 1) * 512],
                        mbc[:, 0:512], ALU.add),
                        w=_mx(tv1, wxt), inc=True)
                    tk_mbc_free = tv2
                    t_nch = _mx(t_nch, tv2)
                tk_x1[ti] = DMA("sync", x1s[ti], xst[:], waits=t_nch, sem="dSP")
                tk_xst_free = tk_x1[ti]
                if stage == 1:
                    tk_xst_free = _mx(tk_x1[ti],
                                      DMA("sync", y_out[b, qt * 128:(qt + 1) * 128, :],
                                          xst[:], waits=t_nch))
                pump_tick()

        pump_flush()

        if stage >= 2:
            pump_final = (("dPO0", sch.count["dPO0"]), ("dPO1", sch.count["dPO1"]))
            spill_final = (("dSP", sch.count["dSP"]),)
            ne_run = int(os.environ.get("MK_NE", str(NE)))
            f1_only = os.environ.get("MK_F1ONLY", "0") == "1"
            f1lvl = int(os.environ.get("MK_F1LVL", "3"))
            barrier_A = sch.barrier_token()
            tk_h_free = ()
            tk_w13buf = [(), ()]
            tk_w2buf = ()
            tk_gbc_free = ()
            tk_t123_free = ()
            tk_gts_free = ()
            tk_x1a_free = ()
            tk_x1b_free = ()
            tk_fstage_free_F = ()
            t_acc_prev = {tt: () for tt in range(8)}

            for c in range(NCHUNK):
                t_ln_done = ()
                t_gts = ()
                for tt in range(8):
                    ti = c * 8 + tt
                    if tt % 2 == 0:
                        xbuf, xfree = xst, tk_x1a_free
                    else:
                        xbuf, xfree = pmpi, tk_x1b_free
                    wxt = DMA("sync", xbuf[:], x1s[ti],
                              waits=_mx(xfree, barrier_A, spill_final,
                                        tk_xst_free if (c == 0 and tt == 0) else ()),
                              sem="dX" if tt % 2 == 0 else "dX2")
                    ln = emit_ln(xbuf[:], (wxt,), tt % 2, junk_guard=tk_junk_guard,
                                 want_tm=False)
                    tk_junk_guard = ln["sq"]
                    tf32 = V(lambda e, xbuf=xbuf, c0=ln["c0"], c1=ln["c1"]:
                        e.tensor_scalar(xbuf[:], xbuf[:], stats[:, c0:c0 + 1],
                                        stats[:, c1:c1 + 1], ALU.add, ALU.mult),
                        w=(ln["rs"],), inc=True)
                    xbuf_last_read = tf32
                    tev_all = ()
                    for r in (range(4) if f1lvl >= 1 else range(0)):
                        bk = 6 + r % 2
                        tks = []
                        for i in range(4):
                            dt = r * 4 + i
                            tks.append(T(lambda e, dt=dt, i=i, bk=bk, xbuf=xbuf:
                                e.transpose(bank(bk, 128, i * 128),
                                            xbuf[:, dt * 128:(dt + 1) * 128],
                                            idf_s[:]),
                                w=_mx(tf32, tk_bank[bk], tCN) if i == 0 else (),
                                inc=(i == 3)))
                        tk_bankw[bk] = tks[-1]
                        tev = V(lambda e, r=r, tt=tt, bk=bk: e.tensor_copy(
                            big_a32[:, r * 4:(r + 1) * 4, tt * 128:tt * 128 + 128],
                            bank(bk).rearrange("p (a b) -> p a b", a=4)),
                            w=(tks[-1],), inc=True)
                        tev2 = SC(lambda e, r=r, bk=bk: e.activation(
                            fstage[:, r * 512:(r + 1) * 512], bank(bk), AF.Copy),
                            w=_mx(tks[-1], tev,
                                  tk_fstage_free_F if r == 0 else ()),
                            inc=True)
                        tk_bank[bk] = _mx(tev, tev2)
                        tev_all = _mx(tev_all, tev, tev2)
                        xbuf_last_read = _mx(xbuf_last_read, tks[-1])
                    t_ln_done = _mx(t_ln_done, tev_all)
                    if tt % 2 == 0:
                        tk_x1a_free = xbuf_last_read
                    else:
                        tk_x1b_free = xbuf_last_read
                    tks = []
                    for dt in (range(NDT) if f1lvl >= 2 else range(0)):
                        tks.append(T(lambda e, dt=dt: e.matmul(
                            bank(5, 8), fstage[:, dt * 128:(dt + 1) * 128],
                            gw_s[:, dt * 8:(dt + 1) * 8],
                            start=(dt == 0), stop=(dt == NDT - 1)),
                            w=_mx(tev_all, tk_bank[5], tCN) if dt == 0 else (),
                            inc=(dt == NDT - 1)))
                    if f1lvl < 2:
                        continue
                    tk_bankw[5] = tks[-1]
                    tk_fstage_free_F = tks[-1]
                    tsg = SC(lambda e: e.activation(g8[:, 16:24], bank(5, 8),
                                                    AF.Sigmoid),
                             w=_mx(tks[-1], tk_gts_free), inc=True)
                    tk_bank[5] = tsg
                    if f1lvl < 3:
                        tk_gts_free = tsg
                        continue
                    tb = V(lambda e: e.tensor_tensor(g8[:, 24:32], g8[:, 16:24],
                                                     eb_s[:, 0:8], ALU.add),
                           w=_mx(tsg, tEB), inc=True)
                    tm1 = V(lambda e: e.reduce_max(g8[:, 32:33], g8[:, 24:32],
                                                   axis=AX.X), w=(tb,), inc=True)
                    te1 = V(lambda e: e.tensor_scalar(g8[:, 40:48], g8[:, 24:32],
                                                      g8[:, 32:33], None,
                                                      ALU.is_equal),
                            w=(tm1,), inc=True)
                    tr1 = V(lambda e: e.tensor_tensor(g8[:, 40:48], g8[:, 40:48],
                                                      rev_s[:, 0:8], ALU.mult),
                            w=_mx(te1, tCN), inc=True)
                    tmr = V(lambda e: e.reduce_max(g8[:, 33:34], g8[:, 40:48],
                                                   axis=AX.X), w=(tr1,), inc=True)
                    tf1 = V(lambda e: e.tensor_scalar(g8[:, 40:48], g8[:, 40:48],
                                                      g8[:, 33:34], None,
                                                      ALU.is_equal),
                            w=(tmr,), inc=True)
                    tb2 = V(lambda e: e.scalar_tensor_tensor(g8[:, 48:56],
                                                             g8[:, 40:48], -1.0e9,
                                                             g8[:, 24:32],
                                                             ALU.mult, ALU.add),
                            w=(tf1,), inc=True)
                    tm2 = V(lambda e: e.reduce_max(g8[:, 34:35], g8[:, 48:56],
                                                   axis=AX.X), w=(tb2,), inc=True)
                    te2 = V(lambda e: e.tensor_scalar(g8[:, 48:56], g8[:, 48:56],
                                                      g8[:, 34:35], None,
                                                      ALU.is_equal),
                            w=(tm2,), inc=True)
                    tr2 = V(lambda e: e.tensor_tensor(g8[:, 48:56], g8[:, 48:56],
                                                      rev_s[:, 0:8], ALU.mult),
                            w=(te2,), inc=True)
                    tmr2 = V(lambda e: e.reduce_max(g8[:, 35:36], g8[:, 48:56],
                                                    axis=AX.X), w=(tr2,), inc=True)
                    tf2 = V(lambda e: e.tensor_scalar(g8[:, 48:56], g8[:, 48:56],
                                                      g8[:, 35:36], None,
                                                      ALU.is_equal),
                            w=(tmr2,), inc=True)
                    tmsk = V(lambda e: e.tensor_tensor(g8[:, 56:64], g8[:, 40:48],
                                                       g8[:, 48:56], ALU.max),
                             w=(tf2,), inc=True)
                    tgsc = V(lambda e: e.tensor_tensor(g8[:, 56:64], g8[:, 56:64],
                                                       g8[:, 16:24], ALU.mult),
                             w=(tmsk,), inc=True)
                    ttr = T(lambda e: e.transpose(arena_ap[0:8, 2560:2688],
                                                  g8[:, 56:64], idf_s[:]),
                            w=_mx(tgsc, tk_bank[5]), inc=True)
                    tk_bankw[5] = ttr
                    tgv = V(lambda e, tt=tt: e.tensor_copy(
                        gts[:, tt * 128:(tt + 1) * 128], arena_ap[0:8, 2560:2688]),
                        w=_mx(ttr, tk_gts_free), inc=True)
                    tk_bank[5] = tgv
                    t_gts = _mx(t_gts, tgv)

                if dbg_mode == "w13" and c == 0:
                    for _j in range(4):
                        DMA("sync", dbg_out[:, _j * 4096:(_j + 1) * 4096],
                            w13s[8, _j], waits=_mx(t_ln_done))
                if dbg_mode == "ln2" and c == 0:
                    DMA("sync", dbg_out[:, 0:16384], big[:, 0:16384],
                        waits=_mx(t_ln_done))
                t_gdma_prev = ()
                t_acc_last = dict(t_acc_prev)
                elist = (list(range(ne_run)) if not f1_only else [])
                if os.environ.get("MK_ELIST"):
                    elist = [int(t) for t in os.environ["MK_ELIST"].split(",")]
                for eidx, ei in enumerate(elist):
                    routed = ei < E
                    if routed:
                        tgb = []
                        for ch in range(2):
                            bk = 6 + ch
                            tmm = T(lambda e, ei=ei, ch=ch, bk=bk: e.matmul(
                                bank(bk), sel8_s[:, ei * 128:(ei + 1) * 128],
                                gts[0:8, ch * 512:(ch + 1) * 512],
                                start=True, stop=True),
                                w=_mx(t_gts, tk_bank[bk]), inc=True)
                            tk_bankw[bk] = tmm
                            t_gdma_prev = _mx(t_gdma_prev, tmm)
                            tcpg = V(lambda e, ch=ch, bk=bk: e.tensor_copy(
                                qkvr[:, 3072 + ch * 512: 3072 + (ch + 1) * 512],
                                bank(bk)), w=_mx(tmm, tk_gbc_free), inc=True)
                            tk_bank[bk] = tcpg
                            tgb.append(tcpg)
                        tGB = _mx(*tgb)
                        tk_gbc_free = ()
                    else:
                        tGB = ()
                    tw2 = None
                    t_h_done = ()
                    for jt in range(NJT):
                        par = jt % 2
                        w13o = 24576 + par * 4096
                        twd = DMA("sync", big[:, w13o:w13o + 4096], w13s[ei, jt],
                                  waits=_mx(tk_w13buf[par], pump_final,
                                            barrier_A if (c == 0 and eidx == 0 and jt < 2) else ()),
                                  sem="dF%d" % par)
                        if jt == 2:
                            tw2 = DMA("sync", zreg[:, 16384:32768], w2s[ei],
                                      waits=_mx(tk_w2buf, pump_final,
                                                barrier_A if (c == 0 and eidx == 0) else ()),
                                      sem="dW2")
                        t_half = ()
                        for hf in range(2):
                            ubk = hf % 2
                            wk_ = 2 + hf % 2
                            tku = []
                            for dt in range(NDT):
                                tku.append(T(lambda e, dt=dt, hf=hf, ubk=ubk, w13o=w13o:
                                    e.matmul(bank(ubk),
                                             big[:, w13o + dt * 128:
                                                 w13o + (dt + 1) * 128],
                                             big[:, dt * 1024 + hf * 512:
                                                 dt * 1024 + hf * 512 + 512],
                                             start=(dt == 0), stop=(dt == NDT - 1)),
                                    w=_mx(twd, tk_bank[ubk],
                                          t_ln_done) if dt == 0 else (),
                                    inc=(dt == NDT - 1)))
                            tkw = []
                            for dt in range(NDT):
                                tkw.append(T(lambda e, dt=dt, hf=hf, wk_=wk_, w13o=w13o:
                                    e.matmul(bank(wk_),
                                             big[:, w13o + 2048 + dt * 128:
                                                 w13o + 2048 + (dt + 1) * 128],
                                             big[:, dt * 1024 + hf * 512:
                                                 dt * 1024 + hf * 512 + 512],
                                             start=(dt == 0), stop=(dt == NDT - 1)),
                                    w=_mx(tk_bank[wk_],) if dt == 0 else (),
                                    inc=(dt == NDT - 1)))
                            tk_bankw[ubk] = tku[-1]
                            tk_bankw[wk_] = tkw[-1]
                            ho = 16384 + jt * 1024 + hf * 512
                            if routed:
                                tt1 = V(lambda e, ubk=ubk, hf=hf: e.tensor_tensor(
                                    qkvr[:, 0:512], bank(ubk),
                                    qkvr[:, 3072 + hf * 512: 3072 + (hf + 1) * 512],
                                    ALU.mult),
                                    w=_mx(tku[-1], tGB, tk_t123_free), inc=True)
                                ts1 = SC(lambda e: e.activation(
                                    qkvr[:, 1024:1536], qkvr[:, 0:512], AF.Silu),
                                    w=(tt1,), inc=True)
                                tt3 = V(lambda e, wk_=wk_, hf=hf: e.tensor_tensor(
                                    qkvr[:, 2048:2560], bank(wk_),
                                    qkvr[:, 3072 + hf * 512: 3072 + (hf + 1) * 512],
                                    ALU.mult), w=_mx(tkw[-1], tGB), inc=True)
                                th = V(lambda e, ho=ho: e.tensor_tensor(
                                    big[:, ho:ho + 512], qkvr[:, 1024:1536],
                                    qkvr[:, 2048:2560], ALU.mult),
                                    w=_mx(ts1, tt3, tk_h_free), inc=True)
                                tk_bank[ubk] = _mx(tk_bank[ubk], tt1)
                                tk_bank[wk_] = _mx(tk_bank[wk_], tt3)
                                tk_t123_free = th
                            else:
                                ts1 = SC(lambda e, ubk=ubk: e.activation(
                                    qkvr[:, 1024:1536], bank(ubk), AF.Silu),
                                    w=_mx(tku[-1], tk_t123_free), inc=True)
                                th = V(lambda e, ho=ho, wk_=wk_: e.tensor_tensor(
                                    big[:, ho:ho + 512], qkvr[:, 1024:1536],
                                    bank(wk_), ALU.mult),
                                    w=_mx(ts1, tkw[-1], tk_h_free), inc=True)
                                tk_bank[ubk] = _mx(tk_bank[ubk], ts1)
                                tk_bank[wk_] = _mx(tk_bank[wk_], th)
                                tk_t123_free = th
                            t_half = _mx(t_half, th)
                        tk_w13buf[par] = t_half
                        t_h_done = _mx(t_h_done, t_half)
                    for tt in range(8):
                        bks = [0, 1, 2, 3] if tt % 2 == 0 else [4, 5, 6, 7]
                        first_w = _mx(t_h_done, tw2, *[tk_bank[bk] for bk in bks])
                        tks_n = {n: [] for n in range(4)}
                        for jt in range(NJT):
                            for nn in range(4):
                                bk = bks[nn]
                                tks_n[nn].append(T(
                                    lambda e, jt=jt, tt=tt, nn=nn, bk=bk: e.matmul(
                                        bank(bk),
                                        big[:, 16384 + jt * 1024 + tt * 128:
                                            16384 + jt * 1024 + tt * 128 + 128],
                                        zreg[:, 16384 + jt * 2048 + nn * 512:
                                             16384 + jt * 2048 + (nn + 1) * 512],
                                        start=(jt == 0), stop=(jt == NJT - 1)),
                                    w=first_w if (jt == 0 and nn == 0) else (),
                                    inc=(jt == NJT - 1)))
                        evs = []
                        for nn in range(4):
                            bk = bks[nn]
                            tk_bankw[bk] = tks_n[nn][-1]
                            ao = tt * 2048 + nn * 512
                            if eidx == 0:
                                tev = V(lambda e, ao=ao, bk=bk: e.tensor_copy(
                                    zreg[:, ao:ao + 512], bank(bk)),
                                    w=_mx(tks_n[nn][-1], t_acc_last[tt]), inc=True)
                            else:
                                tev = V(lambda e, ao=ao, bk=bk: e.tensor_tensor(
                                    zreg[:, ao:ao + 512], bank(bk),
                                    zreg[:, ao:ao + 512], ALU.add),
                                    w=_mx(tks_n[nn][-1], t_acc_last[tt]), inc=True)
                            tk_bank[bk] = tev
                            evs.append(tev)
                        t_acc_last[tt] = _mx(*evs)
                    tk_h_free = _mx(*[t_acc_last[tt] for tt in range(8)])
                    tk_w2buf = tk_h_free
                    tk_gbc_free = t_h_done

                if dbg_mode == "hacc" and c == 0:
                    DMA("sync", dbg_out[:, 0:8192], big[:, 16384:24576],
                        waits=_mx(*[t_acc_last[tt] for tt in range(8)]))
                    DMA("sync", dbg_out[:, 8192:24576],
                        zreg[:, 0:16384],
                        waits=_mx(*[t_acc_last[tt] for tt in range(8)]))
                for tt in range(8):
                    ti = c * 8 + tt
                    bseq = ti // NT
                    trow_i = ti % NT
                    if tt % 2 == 0:
                        xbuf, xfree = xst, tk_x1a_free
                    else:
                        xbuf, xfree = pmpi, tk_x1b_free
                    wxt = DMA("sync", xbuf[:], x1s[ti],
                              waits=_mx(xfree, barrier_A, spill_final),
                              sem="dX" if tt % 2 == 0 else "dX2")
                    if (ne_run > 0 or os.environ.get("MK_ELIST")) and not f1_only:
                        tadd = V(lambda e, xbuf=xbuf, tt=tt: e.tensor_tensor(
                            xbuf[:], xbuf[:], zreg[:, tt * 2048:(tt + 1) * 2048],
                            ALU.add), w=_mx(wxt, t_acc_last[tt]), inc=True)
                    else:
                        tadd = V(lambda e, xbuf=xbuf: e.tensor_copy(
                            xbuf[:, 0:4], xbuf[:, 0:4]),
                            w=_mx(wxt,), inc=True)
                    tout = DMA("sync",
                               y_out[bseq, trow_i * 128:(trow_i + 1) * 128, :],
                               xbuf[:], waits=(tadd,),
                               sem="dOA" if tt % 2 == 0 else "dOB")
                    if tt % 2 == 0:
                        tk_x1a_free = tout
                    else:
                        tk_x1b_free = tout
                t_acc_prev = {tt: t_acc_last[tt] for tt in range(8)}
                tk_gts_free = _mx(t_gdma_prev)

        final = tuple((s, sch.count[s]) for s in DMAS if sch.count[s] > 0)
        sch.op("sync", lambda e: e.nop(), waits=final)

        sch.check_deadlock()

        @blk.sync
        def _(eng):
            sch.replay("sync", eng, sems)

        @blk.gpsimd
        def _(eng):
            sch.replay("gpsimd", eng, sems)

        @blk.tensor
        def _(eng):
            sch.replay("tensor", eng, sems)

        @blk.vector
        def _(eng):
            sch.replay("vector", eng, sems)

        @blk.scalar
        def _(eng):
            sch.replay("scalar", eng, sems)

    return nc


def _prep_inputs(inputs):
    f32 = np.float32
    WQ = np.asarray(inputs["L0_attn_W_Q"], f32)
    WK = np.asarray(inputs["L0_attn_W_K"], f32)
    WV = np.asarray(inputs["L0_attn_W_V"], f32)
    WO = np.asarray(inputs["L0_attn_W_O"], f32)

    def qkv_tile(W):
        a = W.transpose(1, 0, 2)
        a = a.reshape(NDT, 128, H, DH)
        a = a.transpose(2, 1, 0, 3)
        return np.ascontiguousarray(a)

    wq = qkv_tile(WQ)
    wk = qkv_tile(WK)
    wv = qkv_tile(WV)
    wo = np.ascontiguousarray(WO.reshape(H, 128, D))

    bqkv = np.zeros((128, 48), f32)
    bqkv[:, 0:16] = np.asarray(inputs["L0_attn_b_Q"], f32).T
    bqkv[:, 16:32] = np.asarray(inputs["L0_attn_b_K"], f32).T
    bqkv[:, 32:48] = np.asarray(inputs["L0_attn_b_V"], f32).T
    bo = np.ascontiguousarray(np.asarray(inputs["L0_attn_b_O"], f32).reshape(1, D))

    gwm = np.asarray(inputs["L0_ffn_gate_w"], f32)
    gw = np.ascontiguousarray(gwm.T.reshape(NDT, 128, E).transpose(1, 0, 2)
                              .reshape(128, NDT * E))
    eb = np.ascontiguousarray(np.asarray(inputs["L0_ffn_expert_bias"], f32)
                              .reshape(1, E))

    w1 = np.asarray(inputs["L0_ffn_w1"], f32)
    w3 = np.asarray(inputs["L0_ffn_w3"], f32)
    w2m = np.asarray(inputs["L0_ffn_w2"], f32)
    sw1 = np.asarray(inputs["L0_ffn_sw1"], f32)
    sw2 = np.asarray(inputs["L0_ffn_sw2"], f32)
    sw3 = np.asarray(inputs["L0_ffn_sw3"], f32)

    def w13_tile(w):
        a = w.T.reshape(NDT, 128, NJT, 128)
        return a.transpose(2, 1, 0, 3)

    w1b = np.stack([w13_tile(w1[e]) for e in range(E)] + [w13_tile(sw1)])
    w3b = np.stack([w13_tile(w3[e]) for e in range(E)] + [w13_tile(sw3)])
    w13 = np.ascontiguousarray(
        np.stack([w1b, w3b], axis=3).reshape(NE, NJT, 128, 4096))

    def w2_tile(w):
        a = w.T.reshape(NJT, 128, D)
        return a.transpose(1, 0, 2)

    w2 = np.ascontiguousarray(
        np.stack([w2_tile(w2m[e]) for e in range(E)] + [w2_tile(sw2)])
        .reshape(NE, 128, NJT * D))

    x = np.ascontiguousarray(np.asarray(inputs["L0_attn_x"], f32))
    shared = dict(wq=wq, wk=wk, wv=wv, wo=wo, bqkv=bqkv, bo=bo, gw=gw, eb=eb,
                  w13=w13, w2=w2)
    return x, shared


_NC_CACHE = {}


def kernel(**inputs):
    stage = int(os.environ.get("MK_STAGE", "3"))
    stage_key = (stage, os.environ.get("MK_NE", ""), os.environ.get("MK_F1ONLY", ""),
                 os.environ.get("MK_F1LVL", ""), os.environ.get("MK_ELIST", ""),
                 os.environ.get("MK_DBG", ""))
    x, shared = _prep_inputs(inputs)
    if stage_key not in _NC_CACHE:
        _NC_CACHE[stage_key] = build_program(stage=stage)
    nc = _NC_CACHE[stage_key]
    in_maps = []
    for c in range(NCORES):
        m = dict(shared)
        m["x"] = np.ascontiguousarray(x[c * BPC:(c + 1) * BPC])
        in_maps.append(m)
    res = run_bass_kernel_spmd(nc, in_maps, core_ids=list(range(NCORES)))
    if os.environ.get("MK_DBG"):
        kernel.dbg = [r["dbg"] for r in res.results]
    return np.concatenate([r["y"] for r in res.results], axis=0)


if __name__ == "__main__":
    import time
    t0 = time.time()
    nc = build_program(stage=int(os.environ.get("MK_STAGE", "3")))
    print(f"built ok in {time.time() - t0:.1f}s")


# version 10 (final)
# speedup vs baseline: 1.0589x; 1.0458x over previous
import os
from contextlib import ExitStack

import numpy as np
import ml_dtypes
import concourse.bass as bass
import concourse.mybir as mybir
from concourse.bass_utils import run_bass_kernel_spmd

F32 = mybir.dt.float32
BF16 = mybir.dt.bfloat16
AF = mybir.ActivationFunctionType
ALU = mybir.AluOpType
AX = mybir.AxisListType

BATCH, S, D, H, DH, E, HID = 16, 2048, 2048, 16, 128, 8, 1024
NCORES = 8
BPC = BATCH // NCORES
NT = S // 128
NDT = D // 128
NJT = HID // 128
TC = 1024
NCHUNK = (BPC * S) // TC
NE = E + 1
ATTN_SCALE = np.float32(np.sqrt(np.float32(DH)))
CINV = float(np.float32(1.0) / ATTN_SCALE)
SQRT_DH = float(ATTN_SCALE)
LN_EPS = 1e-5
MASK_NEG = -2.0e6

ENG = ("sync", "gpsimd", "tensor", "vector", "scalar")
DMAS = ("dS", "dC", "dR", "dW", "dX", "dX2", "dPI0", "dPI1", "dPO0", "dPO1",
        "dSP", "dF0", "dF1", "dW2", "dOA", "dOB")
SEMS = DMAS + ("tS", "vS", "sS", "gS")
ESEM = {"tensor": "tS", "vector": "vS", "scalar": "sS", "gpsimd": "gS"}


class Sched:
    def __init__(self):
        self.ops = []
        self.count = {s: 0 for s in SEMS}

    def dma(self, queue, dst, src, waits=(), sem="dS"):
        self.count[sem] += 16
        tok = (sem, self.count[sem])
        self.ops.append((queue, _mx(waits),
                         (lambda eng, dst=dst, src=src: eng.dma_start(dst, src)),
                         sem, 16))
        return tok

    def op(self, engine, emit_fn, waits=(), inc=False):
        sem = ESEM[engine] if inc else None
        tok = None
        if sem:
            self.count[sem] += 1
            tok = (sem, self.count[sem])
        self.ops.append((engine, _mx(waits), emit_fn, sem, 1 if sem else 0))
        return tok

    def barrier_token(self):
        return tuple((s, v) for s, v in self.count.items() if v > 0)

    def check_deadlock(self):
        qs = {e: [] for e in ENG}
        for i, (eng, waits, _f, sem, amt) in enumerate(self.ops):
            qs[eng].append((i, waits, sem, amt))
        heads = {e: 0 for e in ENG}
        sem_val = {s: 0 for s in SEMS}
        progress = True
        while progress:
            progress = False
            for e in ENG:
                while heads[e] < len(qs[e]):
                    _i, waits, sem, amt = qs[e][heads[e]]
                    if all(sem_val[s] >= v for s, v in waits):
                        if sem:
                            sem_val[sem] += amt
                        heads[e] += 1
                        progress = True
                    else:
                        break
        stuck = {e: len(qs[e]) - heads[e] for e in ENG if heads[e] < len(qs[e])}
        if stuck:
            detail = []
            for e in ENG:
                if heads[e] < len(qs[e]):
                    i, waits, _s, _a = qs[e][heads[e]]
                    unmet = [(s, v, sem_val[s]) for s, v in waits if sem_val[s] < v]
                    detail.append(f"{e}@op{i} needs {unmet}")
            raise RuntimeError(f"schedule deadlock: {stuck}; " + "; ".join(detail))

    def replay(self, name, eng, sems):
        for engine, waits, emit_fn, sem, amt in self.ops:
            if engine != name:
                continue
            for s, v in waits:
                eng.wait_ge(sems[s], v)
            inst = emit_fn(eng)
            if sem:
                inst.then_inc(sems[sem], amt)


def _mx(*tokens):
    best = {}
    for t in tokens:
        if not t:
            continue
        if isinstance(t[0], str):
            t = (t,)
        for s, v in t:
            if v > best.get(s, -1):
                best[s] = v
    return tuple(best.items())


def build_program(stage=3):
    nc = bass.Bass("TRN2", target_bir_lowering=False, debug=False)

    x_in = nc.dram_tensor("x", [BPC, S, D], F32, kind="ExternalInput").ap()
    wq_in = nc.dram_tensor("wq", [H, 128, NDT, DH], F32, kind="ExternalInput").ap()
    wk_in = nc.dram_tensor("wk", [H, 128, NDT, DH], F32, kind="ExternalInput").ap()
    wv_in = nc.dram_tensor("wv", [H, 128, NDT, DH], F32, kind="ExternalInput").ap()
    wo_in = nc.dram_tensor("wo", [H, 128, D], F32, kind="ExternalInput").ap()
    bqkv_in = nc.dram_tensor("bqkv", [128, 48], F32, kind="ExternalInput").ap()
    bo_in = nc.dram_tensor("bo", [1, D], F32, kind="ExternalInput").ap()
    gw_in = nc.dram_tensor("gw", [128, NDT * E], F32, kind="ExternalInput").ap()
    eb_in = nc.dram_tensor("eb", [1, E], F32, kind="ExternalInput").ap()
    w13_in = nc.dram_tensor("w13", [NE, NJT, 128, 4096], F32, kind="ExternalInput").ap()
    w2_in = nc.dram_tensor("w2", [NE, 128, NJT * D], F32, kind="ExternalInput").ap()
    y_out = nc.dram_tensor("y", [BPC, S, D], F32, kind="ExternalOutput").ap()
    dbg_mode = os.environ.get("MK_DBG", "")
    dbg_out = (nc.dram_tensor("dbg", [128, 32768], BF16, kind="ExternalOutput").ap()
               if dbg_mode else None)

    w13s = nc.dram_tensor("w13s", [NE, NJT, 128, 4096], BF16, kind="Internal").ap()
    w2s = nc.dram_tensor("w2s", [NE, 128, NJT * D], BF16, kind="Internal").ap()
    x1s = nc.dram_tensor("x1s", [BPC * NT, 128, D], F32, kind="Internal").ap()

    ident_bf = nc.inline_tensor(np.eye(128, dtype=ml_dtypes.bfloat16), name="idbf").ap()
    ident_f = nc.inline_tensor(np.eye(128, dtype=np.float32), name="idf").ap()
    tril = np.tril(np.ones((128, 128), bool))
    minfl = nc.inline_tensor(
        np.where(tril, np.float32(3.0e38), np.float32(MASK_NEG)).astype(np.float32),
        name="minfl").ap()
    minfu = nc.inline_tensor(
        np.where(tril.T, np.float32(3.0e38), np.float32(MASK_NEG)).astype(np.float32),
        name="minfu").ap()
    ones1 = nc.inline_tensor(np.ones((1, 128), np.float32), name="ones1").ap()
    rev_c = nc.inline_tensor(
        np.tile(np.arange(E, 0, -1, dtype=np.float32)[None, :], (128, 1)),
        name="revc").ap()
    ones2_c = nc.inline_tensor(np.ones((2, 128), ml_dtypes.bfloat16), name="ones2").ap()
    sel8_np = np.zeros((8, 8, 128), ml_dtypes.bfloat16)
    for _e in range(8):
        sel8_np[_e, _e, :] = 1.0
    sel8_c = nc.inline_tensor(np.ascontiguousarray(
        sel8_np.transpose(1, 0, 2).reshape(8, 1024)), name="sel8").ap()
    eps_c = nc.inline_tensor(np.full((128, 1), LN_EPS, np.float32), name="epsc").ap()

    sch = Sched()
    stk = ExitStack()
    sb = lambda name, shape, dt: stk.enter_context(nc.sbuf_tensor(name, shape, dt))

    with stk:
        big = sb("big", [128, 32768], BF16)
        zreg = sb("zreg", [128, 32768], BF16)
        qkvr = sb("qkvr", [128, 4096], F32)
        fstage = sb("fstage", [128, 2048], F32)
        wbf = sb("wbf", [128, 4096], BF16)
        xst = sb("xst", [128, 2048], F32)
        pmpi = sb("pmpi", [128, 2048], F32)
        pmpo = sb("pmpo", [128, 2048], BF16)
        vtm2 = sb("vtm2", [128, 2048], BF16)
        mbc = sb("mbc", [128, 1024], F32)
        ebuf = sb("ebuf", [128, 1024], BF16)
        junkb = sb("junkb", [128, 512], BF16)
        stats = sb("stats", [128, 64], F32)
        row1 = sb("row1", [1, 1024], F32)
        gts = sb("gts", [8, TC], BF16)
        sel8_s = sb("sel8_s", [8, 1024], BF16)
        ones2_s = sb("ones2_s", [2, 128], BF16)
        g8 = sb("g8", [128, 64], F32)
        idb_s = sb("idb_s", [128, 128], BF16)
        idf_s = sb("idf_s", [128, 128], F32)
        mfl_s = sb("mfl_s", [128, 128], F32)
        mfu_s = sb("mfu_s", [128, 128], F32)
        ones_s = sb("ones_s", [1, 128], F32)
        bqkv_s = sb("bqkv_s", [128, 48], F32)
        bo_bf = sb("bo_bf", [128, 2048], BF16)
        gw_s = sb("gw_s", [128, 128], F32)
        eb_s = sb("eb_s", [128, 8], F32)
        rev_s = sb("rev_s", [128, 8], F32)
        eps_s = sb("eps_s", [128, 1], F32)
        arena = stk.enter_context(nc.psum_tensor("arena", [128, 4096], F32))

        sems = {s: stk.enter_context(nc.semaphore(s)) for s in SEMS}
        blk = stk.enter_context(nc.Block())

        arena_ap = arena[:]
        arena_bf = arena_ap.bitcast(BF16)

        def bank(i, w=512, off=0):
            return arena_ap[:, i * 512 + off: i * 512 + off + w]

        def bank_bf(i, w=1024, off=0):
            return arena_bf[:, i * 1024 + off: i * 1024 + off + w]

        qkv_bf = qkvr[:].bitcast(BF16)
        mbc_bf = mbc[:].bitcast(BF16)
        def echunk(epar, wdt=512):
            if epar < 2:
                return ebuf[:, epar * 512: epar * 512 + wdt]
            return mbc_bf[:, (epar - 2) * 512: (epar - 2) * 512 + wdt]
        qh = qkv_bf[:, 0:2048]
        kh = qkv_bf[:, 2048:4096]
        vfm = qkv_bf[:, 4096:6144]
        vtm = qkv_bf[:, 6144:8192]
        xst_bf = None
        big_a16 = big[:].rearrange("p (a b) -> p a b", a=16)
        big_a32 = big[:].rearrange("p (a b) -> p a b", a=32)

        T = lambda f, w=(), inc=False: sch.op("tensor", f, w, inc)
        V = lambda f, w=(), inc=False: sch.op("vector", f, w, inc)
        SC = lambda f, w=(), inc=False: sch.op("scalar", f, w, inc)
        DMA = sch.dma

        for _dst, _src in ((idb_s[:], ident_bf), (idf_s[:], ident_f),
                           (mfl_s[:], minfl), (mfu_s[:], minfu),
                           (ones_s[:], ones1), (bqkv_s[:], bqkv_in),
                           (rev_s[:], rev_c), (gw_s[:], gw_in),
                           (eps_s[:], eps_c), (sel8_s[:], sel8_c),
                           (ones2_s[:], ones2_c)):
            DMA("sync", _dst, _src, sem="dC")
        tCN = (("dC", sch.count["dC"]),)

        tk_bo = []
        prev_cp = ()
        for hh in range(2):
            tdr = DMA("sync", row1[0:1, :], bo_in[0:1, hh * 1024:(hh + 1) * 1024],
                      waits=_mx(prev_cp), sem="dR")
            for ch in range(2):
                tmm = T(lambda e, hh=hh, ch=ch: e.matmul(
                    bank(6), ones_s[0:1, :], row1[0:1, ch * 512:(ch + 1) * 512],
                    start=True, stop=True), w=_mx(tdr, prev_cp, tCN), inc=True)
                prev_cp = V(lambda e, hh=hh, ch=ch: e.tensor_copy(
                    bo_bf[:, hh * 1024 + ch * 512: hh * 1024 + (ch + 1) * 512],
                    bank(6)), w=(tmm,), inc=True)
                tk_bo.append(prev_cp)
        tdr = DMA("sync", row1[0:1, 0:8], eb_in, waits=_mx(prev_cp), sem="dR")
        tmm = T(lambda e: e.matmul(bank(6, 8), ones_s[0:1, :], row1[0:1, 0:8],
                                   start=True, stop=True),
                w=_mx(tdr, prev_cp), inc=True)
        tEB = V(lambda e: e.tensor_copy(eb_s[:, 0:8], bank(6, 8)), w=(tmm,), inc=True)
        tBO = _mx(*tk_bo)
        tk_row1_free = tEB
        tk_bank = {i: () for i in range(8)}
        tk_bank[6] = _mx(tEB, tBO)

        pump_steps = []
        for ee in range(NE):
            for jt in range(NJT):
                for qq in range(4):
                    pump_steps.append((w13_in[ee, jt, :, qq * 1024:(qq + 1) * 1024],
                                       w13s[ee, jt, :, qq * 1024:(qq + 1) * 1024],
                                       ("w13", ee, jt)))
            for qq in range(16):
                pump_steps.append((w2_in[ee, :, qq * 1024:(qq + 1) * 1024],
                                   w2s[ee, :, qq * 1024:(qq + 1) * 1024],
                                   ("w2", ee)))
        pump_tok = {}
        pstate = {"i": 0, "in": [(), ()], "cast": [(), ()], "out": [(), ()]}

        def pump_issue_in(i):
            if i >= len(pump_steps):
                return
            src = pump_steps[i][0]
            par = i % 2
            pstate["in"][par] = DMA("sync", pmpi[:, par * 1024:(par + 1) * 1024],
                                    src, waits=_mx(pstate["cast"][par]),
                                    sem="dPI%d" % par)

        def pump_tick():
            i = pstate["i"]
            if i >= len(pump_steps):
                return
            pstate["i"] = i + 1
            if i == 0:
                pump_issue_in(0)
                pump_issue_in(1)
            par = i % 2
            dst, key = pump_steps[i][1], pump_steps[i][2]
            if (i // 2) % 2 == 0:
                tcast = SC(lambda e, par=par: e.activation(
                    pmpo[:, par * 1024:(par + 1) * 1024],
                    pmpi[:, par * 1024:(par + 1) * 1024], AF.Copy),
                    w=_mx(pstate["in"][par], pstate["out"][par]), inc=True)
            else:
                tcast = V(lambda e, par=par: e.tensor_copy(
                    pmpo[:, par * 1024:(par + 1) * 1024],
                    pmpi[:, par * 1024:(par + 1) * 1024]),
                    w=_mx(pstate["in"][par], pstate["out"][par]), inc=True)
            pstate["cast"][par] = tcast
            tout = DMA("sync", dst, pmpo[:, par * 1024:(par + 1) * 1024],
                       waits=(tcast,), sem="dPO%d" % par)
            pstate["out"][par] = tout
            pump_tok[key] = tout
            pump_issue_in(i + 2)

        def pump_flush():
            while pstate["i"] < len(pump_steps):
                pump_tick()

        def emit_ln(xsrc, wx, par, junk_guard=(), tm_guard=(), want_tm=True):
            c0, c1 = 24 + 3 * par, 25 + 3 * par
            tsum = V(lambda e: e.reduce_sum(stats[:, c0:c0 + 1], xsrc, axis=AX.X),
                     w=wx, inc=True)
            tneg = V(lambda e: e.tensor_scalar_mul(stats[:, c0:c0 + 1],
                                                   stats[:, c0:c0 + 1], -1.0 / D),
                     w=(tsum,), inc=True)
            tsq = SC(lambda e: e.activation(wbf[:, 0:2048], xsrc, AF.Square,
                                            bias=stats[:, c0:c0 + 1],
                                            accum_out=stats[:, c1:c1 + 1]),
                     w=_mx(tneg, wx, junk_guard), inc=True)
            tsd = SC(lambda e: e.activation(stats[:, c1:c1 + 1], stats[:, c1:c1 + 1],
                                            AF.Sqrt, bias=eps_s[:, 0:1], scale=1.0 / D),
                     w=(tsq,), inc=True)
            trs = V(lambda e: e.reciprocal(stats[:, c1:c1 + 1], stats[:, c1:c1 + 1]),
                    w=(tsd,), inc=True)
            ttm = None
            if want_tm:
                ttm = V(lambda e: e.tensor_scalar(wbf[:, 2048:4096], xsrc,
                                                  stats[:, c0:c0 + 1],
                                                  stats[:, c1:c1 + 1],
                                                  ALU.add, ALU.mult),
                        w=_mx(trs, tm_guard), inc=True)
            return {"sum": tsum, "neg": tneg, "sq": tsq, "sd": tsd, "rs": trs,
                    "tm": ttm, "c0": c0, "c1": c1}

        tk_bankw = {i: () for i in range(8)}
        tk_wbf_free = [(), ()]
        tk_fstage_free = ()
        tk_xst_free = ()
        tk_qkv_free = {"q": (), "k": (), "vf": (), "vt": ()}
        tk_ebuf = [(), (), (), ()]
        tk_stats_par = [(), ()]
        tk_mbc_free = ()
        tk_junk_guard = ()
        tk_tm_guard = ()
        tk_x1 = {}
        tk_z_done = {}

        tk_parity_free = {0: (), 1: ()}
        for b in range(BPC):
            tk_junk_guard = _mx(tk_junk_guard, tk_wbf_free[0])
            tk_tm_guard = _mx(tk_tm_guard, tk_wbf_free[1])
            tk_xln = ()
            for tt in range(NT):
                wx = _mx(DMA("sync", xst[:], x_in[b, tt * 128:(tt + 1) * 128, :],
                             waits=tk_xst_free, sem="dX"))
                ln = emit_ln(xst[:], wx, tt % 2, junk_guard=tk_junk_guard,
                             tm_guard=tk_tm_guard)
                tk_junk_guard = ln["sq"]
                tk_xst_free = ln["tm"]
                tr_last = ()
                for r in range(2):
                    bk = 6 + r
                    tks = []
                    for i in range(8):
                        dt = r * 8 + i
                        tks.append(T(lambda e, dt=dt, i=i, bk=bk: e.transpose(
                            bank_bf(bk, 128, i * 128),
                            wbf[:, 2048 + dt * 128: 2048 + (dt + 1) * 128],
                            idb_s[:]),
                            w=_mx(ln["tm"], tk_bank[bk], tCN) if i == 0 else (),
                            inc=(i == 7)))
                    tk_bankw[bk] = tks[-1]
                    tev = V(lambda e, r=r, tt=tt, bk=bk: e.tensor_copy(
                        big_a16[:, r * 8:(r + 1) * 8, tt * 128:tt * 128 + 128],
                        bank_bf(bk).rearrange("p (a b) -> p a b", a=8)),
                        w=(tks[-1],), inc=True)
                    tk_bank[bk] = tev
                    tk_xln = _mx(tk_xln, tev)
                    tr_last = _mx(tr_last, tks[-1])
                tk_tm_guard = tr_last
                pump_tick()

            qk2 = xst[:].bitcast(BF16)
            hstate = {}
            tk_g8_free = {0: (), 1: ()}
            sc_rot = {"i": 0}
            tk_xln_reads = ()
            def next_bk():
                sc_rot["i"] = (sc_rot["i"] + 1) % 4
                return sc_rot["i"]
            tk_xln_cur = tk_xln

            def gen_fills(h):
                nonlocal tk_fstage_free, tk_xln_reads
                par = h % 2
                if par == 0:
                    bq_, bk_, bvt = qh, kh, vtm
                else:
                    bq_, bk_, bvt = qk2[:, 0:2048], qk2[:, 2048:4096], vtm2[:]
                free_tok = tk_parity_free[par]
                projs = (("q", wq_in, 0, bq_, "bq"), ("k", wk_in, 1, bk_, "bk"),
                         ("vf", wv_in, 0, vfm, None))
                tk_fill = {}
                for nm, w_mat, slot, dstbuf, biasnm in projs:
                    wdma = DMA("sync", fstage[:],
                               w_mat[h].rearrange("p a b -> p (a b)"),
                               waits=_mx(tk_fstage_free), sem="dW")
                    guard = tk_junk_guard if slot == 0 else tk_tm_guard
                    tcast = V(lambda e, slot=slot: e.tensor_copy(
                        wbf[:, slot * 2048:(slot + 1) * 2048], fstage[:]),
                        w=_mx(wdma, tk_wbf_free[slot], guard), inc=True)
                    tk_fstage_free = tcast
                    buf_guard = free_tok if nm in ("q", "k") else tk_qkv_free["vf"]
                    fill_last = ()
                    fill_mm_last = ()
                    for chn in range(4):
                        bk = 6 + chn % 2
                        wfirst = _mx(tcast, tk_bank[bk], tk_xln_cur, buf_guard)
                        tks = []
                        for dt in range(NDT):
                            tks.append(T(lambda e, dt=dt, chn=chn, slot=slot, bk=bk:
                                e.matmul(bank(bk),
                                         wbf[:, slot * 2048 + dt * 128:
                                             slot * 2048 + (dt + 1) * 128],
                                         big[:, dt * 2048 + chn * 512:
                                             dt * 2048 + chn * 512 + 512],
                                         start=(dt == 0), stop=(dt == NDT - 1)),
                                w=wfirst if dt == 0 else (), inc=(dt == NDT - 1)))
                        tk_bankw[bk] = tks[-1]
                        fill_mm_last = _mx(fill_mm_last, tks[-1])
                        if biasnm is not None:
                            bcol = {"bq": 0, "bk": 16}[biasnm] + h
                            tev = V(lambda e, chn=chn, dstbuf=dstbuf, bcol=bcol, bk=bk:
                                e.tensor_scalar_add(
                                    dstbuf[:, chn * 512:(chn + 1) * 512],
                                    bank(bk), bqkv_s[:, bcol:bcol + 1]),
                                w=(tks[-1],), inc=True)
                        else:
                            tev = V(lambda e, chn=chn, dstbuf=dstbuf, bk=bk:
                                e.tensor_copy(dstbuf[:, chn * 512:(chn + 1) * 512],
                                              bank(bk)),
                                w=(tks[-1],), inc=True)
                        tk_bank[bk] = _mx(tk_bank[bk], tev)
                        fill_last = _mx(fill_last, tev)
                        yield
                    tk_fill[nm] = fill_last
                    tk_wbf_free[slot] = fill_mm_last
                    pump_tick()
                tvt = ()
                for r in range(2):
                    bk = 6 + r
                    tks = []
                    for i in range(8):
                        tt2 = r * 8 + i
                        tks.append(T(lambda e, tt2=tt2, i=i, bk=bk: e.transpose(
                            bank_bf(bk, 128, i * 128),
                            vfm[:, tt2 * 128:(tt2 + 1) * 128], idb_s[:]),
                            w=_mx(tk_fill["vf"], tk_bank[bk],
                                  free_tok) if i == 0 else (),
                            inc=(i == 7)))
                    tk_bankw[bk] = tks[-1]
                    tev = V(lambda e, r=r, bk=bk, bvt=bvt: e.tensor_copy(
                        bvt[:, r * 1024:(r + 1) * 1024], bank_bf(bk)),
                        w=(tks[-1],), inc=True)
                    tk_bank[bk] = tev
                    tvt = _mx(tvt, tev)
                    yield
                tk_qkv_free["vf"] = tvt
                tk_xln_reads = _mx(tk_xln_reads, tk_fill["q"], tk_fill["k"],
                                   tk_fill["vf"])
                hstate[h] = dict(fq=tk_fill["q"], fk=tk_fill["k"], tvt=tvt,
                                 q=bq_, k=bk_, vt=bvt)

            def gen_pass1(h):
                st = hstate[h]
                bq_, bk_ = st["q"], st["k"]
                goff = 16 * (h % 2)
                t_m2c = []
                for qt in range(NT):
                    par = qt % 2
                    so = 32 * par
                    kend = (qt + 1) * 128
                    nch = (kend + 511) // 512
                    t_sum_toks = []
                    t_m_toks = []
                    guard = tk_stats_par[par]
                    for ch in range(nch):
                        lo = ch * 512
                        wdt = min(512, kend - lo)
                        bk = next_bk()
                        tmm = T(lambda e, qt=qt, lo=lo, wdt=wdt, bk=bk, bq_=bq_, bk_=bk_:
                            e.matmul(bank(bk, wdt), bq_[:, qt * 128:(qt + 1) * 128],
                                     bk_[:, lo:lo + wdt], start=True, stop=True),
                            w=_mx(st["fq"], st["fk"], tk_bank[bk]), inc=True)
                        tk_bankw[bk] = tmm
                        tpre = tmm
                        if lo + wdt == kend:
                            tpre = V(lambda e, bk=bk, wdt=wdt: e.tensor_tensor(
                                bank(bk, 128, wdt - 128), bank(bk, 128, wdt - 128),
                                mfl_s[:], ALU.min), w=_mx(tmm, tCN), inc=True)
                        tmax = V(lambda e, bk=bk, wdt=wdt, so=so, ch=ch: e.reduce_max(
                            stats[:, so + ch:so + ch + 1], bank(bk, wdt), axis=AX.X),
                            w=_mx(tpre, guard), inc=True)
                        tngm = SC(lambda e, so=so, ch=ch: e.mul(
                            stats[:, so + 10 + ch:so + 11 + ch],
                            stats[:, so + ch:so + ch + 1], -CINV),
                            w=_mx(tmax, guard), inc=True)
                        texp = SC(lambda e, bk=bk, wdt=wdt, so=so, ch=ch: e.activation(
                            junkb[:, 0:wdt], bank(bk, wdt), AF.Exp,
                            bias=stats[:, so + 10 + ch:so + 11 + ch], scale=CINV,
                            accum_out=stats[:, so + 4 + ch:so + 5 + ch]),
                            w=(tngm,), inc=True)
                        tk_bank[bk] = _mx(tk_bank[bk], texp)
                        t_sum_toks.append(texp)
                        t_m_toks.append(tmax)
                        yield
                    tm = V(lambda e, so=so, nch=nch: e.reduce_max(
                        stats[:, so + 8:so + 9], stats[:, so:so + nch], axis=AX.X),
                        w=_mx(*t_m_toks), inc=True)
                    tnm = SC(lambda e, so=so: e.mul(
                        stats[:, so + 9:so + 10], stats[:, so + 8:so + 9], -CINV),
                        w=(tm,), inc=True)
                    tec = SC(lambda e, so=so, nch=nch: e.activation(
                        stats[:, so + 14:so + 14 + nch], stats[:, so:so + nch],
                        AF.Exp, bias=stats[:, so + 9:so + 10], scale=CINV),
                        w=_mx(tnm, *t_sum_toks), inc=True)
                    tpr = V(lambda e, so=so, nch=nch: e.tensor_tensor(
                        stats[:, so + 18:so + 18 + nch],
                        stats[:, so + 4:so + 4 + nch],
                        stats[:, so + 14:so + 14 + nch], ALU.mult),
                        w=(tec,), inc=True)
                    tsm = V(lambda e, so=so, nch=nch: e.reduce_sum(
                        stats[:, so + 22:so + 23], stats[:, so + 18:so + 18 + nch],
                        axis=AX.X), w=(tpr,), inc=True)
                    tln = SC(lambda e, so=so: e.activation(
                        stats[:, so + 23:so + 24], stats[:, so + 22:so + 23], AF.Ln),
                        w=(tsm,), inc=True)
                    tmc = V(lambda e, so=so, qt=qt, goff=goff: e.tensor_scalar(
                        g8[:, goff + qt:goff + qt + 1], stats[:, so + 23:so + 24],
                        SQRT_DH, stats[:, so + 8:so + 9], ALU.mult, ALU.add),
                        w=_mx(tln, tk_g8_free[h % 2] if qt < 2 else ()), inc=True)
                    tk_stats_par[par] = _mx(tln, tmc)
                    t_m2c.append(tmc)
                    pump_tick()
                    yield
                hstate[h]["m2c"] = t_m2c


            def gen_pass2(h):
                nonlocal tk_row1_free, tk_mbc_free
                st = hstate[h]
                bq_, bk_, bvt = st["q"], st["k"], st["vt"]
                goff = 16 * (h % 2)
                t_m2c = hstate[h]["m2c"]
                t_z_evac = ()
                ech = 0
                for half in range(2):
                    ttrs = []
                    for j in range(8):
                        qt = half * 8 + j
                        bkj = 2 + j // 4
                        ttrs.append(T(lambda e, j=j, qt=qt, goff=goff: e.transpose(
                            arena_ap[0:1, 1024 + j * 128: 1024 + (j + 1) * 128],
                            g8[:, goff + qt:goff + qt + 1], idf_s[:]),
                            w=_mx(t_m2c[qt], tk_bank[bkj] if j % 4 == 0 else (),
                                  tCN),
                            inc=(j in (3, 7))))
                    tk_bankw[2] = ttrs[3]
                    tk_bankw[3] = ttrs[7]
                    tk_g8_free[h % 2] = _mx(tk_g8_free[h % 2], ttrs[7])
                    thi = V(lambda e: e.tensor_scalar_mul(
                        mbc_bf[0:1, 1024:2048], arena_ap[0:1, 1024:2048], -1.0),
                        w=_mx(ttrs[3], ttrs[7], tk_row1_free), inc=True)
                    tlo = V(lambda e: e.scalar_tensor_tensor(
                        gts[0:1, 0:1024], arena_ap[0:1, 1024:2048], -1.0,
                        mbc_bf[0:1, 1024:2048], ALU.mult, ALU.subtract),
                        w=(thi,), inc=True)
                    tk_bank[2] = _mx(tk_bank[2], tlo)
                    tk_bank[3] = _mx(tk_bank[3], tlo)
                    tdl = DMA("sync", mbc_bf[1:2, 1024:2048], gts[0:1, 0:1024],
                              waits=(tlo,), sem="dR")
                    tMB = _mx(tlo, tdl)
                    tk_row1_free = ()
                    t_mbc_used = ()
                    t_z_last = ()
                    kmax = 8 if half == 0 else 16
                    chunks = []
                    for kt in range(kmax):
                        glo = max(kt * 128, half * 1024)
                        llo = glo - half * 1024
                        cuts = sorted(set([llo] + ([512] if llo < 512 else []) + [1024]))
                        first = True
                        for ci in range(len(cuts) - 1):
                            chunks.append((kt, cuts[ci], cuts[ci + 1] - cuts[ci],
                                           first and glo == kt * 128, kt == 0 and first))
                            first = False

                    pend = None
                    for (kt, lo, wdt, diag, zstart) in chunks:
                        bk = next_bk()
                        epar = ech % 4
                        ech += 1
                        tbias = T(lambda e, lo=lo, wdt=wdt, bk=bk: e.matmul(
                            bank(bk, wdt), ones2_s[0:2, :],
                            mbc_bf[0:2, 1024 + lo: 1024 + lo + wdt],
                            start=True, stop=False),
                            w=_mx(tk_bank[bk], tMB, tCN), inc=False)
                        tmm = T(lambda e, kt=kt, lo=lo, wdt=wdt, half=half, bk=bk,
                                bq_=bq_, bk_=bk_:
                            e.matmul(bank(bk, wdt),
                                     bk_[:, kt * 128:(kt + 1) * 128],
                                     bq_[:, half * 1024 + lo: half * 1024 + lo + wdt],
                                     start=False, stop=True),
                            w=_mx(st["fq"], st["fk"]),
                            inc=True)
                        tk_bankw[bk] = tmm
                        t_mbc_used = _mx(t_mbc_used, tmm)
                        tk_row1_free = _mx(tk_row1_free, tmm)
                        tpre = tmm
                        if diag:
                            tpre = V(lambda e, bk=bk: e.tensor_tensor(
                                bank(bk, 128), bank(bk, 128), mfu_s[:], ALU.min),
                                w=(tmm,), inc=True)
                        texp = SC(lambda e, bk=bk, wdt=wdt, epar=epar:
                            e.activation(echunk(epar, wdt),
                                         bank(bk, wdt), AF.Exp, scale=CINV),
                            w=_mx(tpre, tk_ebuf[epar]), inc=True)
                        tk_bank[bk] = _mx(tk_bank[bk], texp)
                        yield
                        if pend is not None:
                            (pkt, plo, pwdt, pzstart, ptexp, pepar) = pend
                            tzm = T(lambda e, kt=pkt, lo=plo, wdt=pwdt, epar=pepar,
                                    bvt=bvt:
                                e.matmul(arena_ap[:, 2048 + lo: 2048 + lo + wdt],
                                         bvt[:, kt * 128:(kt + 1) * 128],
                                         echunk(epar, wdt),
                                         start=(kt == 0), stop=False),
                                w=_mx(ptexp, st["tvt"],
                                      tk_bank[4] if pzstart else ()),
                                inc=True)
                            tk_ebuf[pepar] = tzm
                            tk_bankw[4] = tzm
                            t_z_last = _mx(t_z_last, tzm)
                        pend = (kt, lo, wdt, zstart, texp, epar)
                        if ech % 4 == 0:
                            pump_tick()
                        yield
                    if pend is not None:
                        (pkt, plo, pwdt, pzstart, ptexp, pepar) = pend
                        tzm = T(lambda e, kt=pkt, lo=plo, wdt=pwdt, epar=pepar,
                                bvt=bvt:
                            e.matmul(arena_ap[:, 2048 + lo: 2048 + lo + wdt],
                                     bvt[:, kt * 128:(kt + 1) * 128],
                                     echunk(epar, wdt),
                                     start=(kt == 0), stop=False),
                            w=_mx(ptexp, st["tvt"],
                                  tk_bank[4] if pzstart else ()),
                            inc=True)
                        tk_ebuf[pepar] = tzm
                        tk_bankw[4] = tzm
                        t_z_last = _mx(t_z_last, tzm)
                        pend = None
                    bvcol = 32 + h
                    tze = V(lambda e, h=h, half=half, bvcol=bvcol: e.tensor_scalar_add(
                        zreg[:, h * 2048 + half * 1024: h * 2048 + (half + 1) * 1024],
                        arena_ap[:, 2048:3072], bqkv_s[:, bvcol:bvcol + 1]),
                        w=_mx(t_z_last,), inc=True)
                    tk_bank[4] = tze
                    tk_bank[5] = tze
                    tk_mbc_free = t_mbc_used
                    t_z_evac = _mx(t_z_evac, tze)
                tk_z_done[(b, h)] = t_z_evac
                tk_parity_free[h % 2] = t_z_evac
                del hstate[h]

            def drive(lanes):
                live = [g for g in lanes if g is not None]
                while live:
                    for g in list(live):
                        try:
                            next(g)
                        except StopIteration:
                            live.remove(g)

            def lane_chain(*gens):
                for g in gens:
                    if g is not None:
                        yield from g

            drive([lane_chain(gen_fills(0), gen_pass1(0))])
            for h in range(H):
                lane2 = (lane_chain(gen_fills(h + 1), gen_pass1(h + 1))
                         if h + 1 < H else None)
                drive([gen_pass2(h), lane2])
            tk_xst_free = _mx(tk_xst_free, tk_parity_free[1])

            tk_allz = _mx(*[tk_z_done[(b, hh)] for hh in range(H)])
            tk_wo = ()
            for ht in range(H):
                wdma = DMA("sync", fstage[:], wo_in[ht], waits=tk_fstage_free,
                           sem="dW")
                tcast = SC(lambda e, ht=ht: e.activation(
                    big[:, ht * 2048:(ht + 1) * 2048], fstage[:], AF.Copy),
                    w=_mx(wdma, tk_xln_reads if ht == 0 else ()), inc=True)
                tk_fstage_free = tcast
                tk_wo = _mx(tk_wo, tcast)
                if ht % 4 == 0:
                    pump_tick()
            for qt in range(NT):
                ti = b * NT + qt
                wxt = DMA("sync", xst[:], x_in[b, qt * 128:(qt + 1) * 128, :],
                          waits=tk_xst_free, sem="dX")
                t_nch = ()
                for chn in range(4):
                    bk = 6 + chn % 2
                    tks = []
                    for ht in range(H):
                        tks.append(T(lambda e, qt=qt, chn=chn, ht=ht, bk=bk:
                            e.matmul(bank(bk),
                                     zreg[:, ht * 2048 + qt * 128:
                                          ht * 2048 + (qt + 1) * 128],
                                     big[:, ht * 2048 + chn * 512:
                                         ht * 2048 + chn * 512 + 512],
                                     start=(ht == 0), stop=(ht == H - 1)),
                            w=_mx(tk_wo, tk_allz, tk_bank[bk]) if ht == 0 else (),
                            inc=(ht == H - 1)))
                    tk_bankw[bk] = tks[-1]
                    tv1 = V(lambda e, chn=chn, bk=bk: e.tensor_tensor(
                        mbc[:, 0:512], bank(bk),
                        bo_bf[:, chn * 512:(chn + 1) * 512], ALU.add),
                        w=_mx(tks[-1], tBO, tk_mbc_free), inc=True)
                    tk_bank[bk] = tv1
                    tk_mbc_free = ()
                    tv2 = V(lambda e, chn=chn: e.tensor_tensor(
                        xst[:, chn * 512:(chn + 1) * 512],
                        xst[:, chn * 512:(chn + 1) * 512],
                        mbc[:, 0:512], ALU.add),
                        w=_mx(tv1, wxt), inc=True)
                    tk_mbc_free = tv2
                    t_nch = _mx(t_nch, tv2)
                tk_x1[ti] = DMA("sync", x1s[ti], xst[:], waits=t_nch, sem="dSP")
                tk_xst_free = tk_x1[ti]
                if stage == 1:
                    tk_xst_free = _mx(tk_x1[ti],
                                      DMA("sync", y_out[b, qt * 128:(qt + 1) * 128, :],
                                          xst[:], waits=t_nch))
                pump_tick()

        pump_flush()

        if stage >= 2:
            pump_final = (("dPO0", sch.count["dPO0"]), ("dPO1", sch.count["dPO1"]))
            spill_final = (("dSP", sch.count["dSP"]),)
            ne_run = int(os.environ.get("MK_NE", str(NE)))
            f1_only = os.environ.get("MK_F1ONLY", "0") == "1"
            f1lvl = int(os.environ.get("MK_F1LVL", "3"))
            barrier_A = sch.barrier_token()
            tk_h_free = ()
            tk_w13buf = [(), ()]
            tk_w2buf = ()
            tk_gbc_free = ()
            tk_t123_free = ()
            tk_gts_free = ()
            tk_x1a_free = ()
            tk_x1b_free = ()
            tk_fstage_free_F = ()
            t_acc_prev = {tt: () for tt in range(8)}

            for c in range(NCHUNK):
                t_ln_done = ()
                t_gts = ()
                for tt in range(8):
                    ti = c * 8 + tt
                    if tt % 2 == 0:
                        xbuf, xfree = xst, tk_x1a_free
                    else:
                        xbuf, xfree = pmpi, tk_x1b_free
                    wxt = DMA("sync", xbuf[:], x1s[ti],
                              waits=_mx(xfree, barrier_A, spill_final,
                                        tk_xst_free if (c == 0 and tt == 0) else ()),
                              sem="dX" if tt % 2 == 0 else "dX2")
                    ln = emit_ln(xbuf[:], (wxt,), tt % 2, junk_guard=tk_junk_guard,
                                 want_tm=False)
                    tk_junk_guard = ln["sq"]
                    tf32 = V(lambda e, xbuf=xbuf, c0=ln["c0"], c1=ln["c1"]:
                        e.tensor_scalar(xbuf[:], xbuf[:], stats[:, c0:c0 + 1],
                                        stats[:, c1:c1 + 1], ALU.add, ALU.mult),
                        w=(ln["rs"],), inc=True)
                    xbuf_last_read = tf32
                    tev_all = ()
                    for r in (range(4) if f1lvl >= 1 else range(0)):
                        bk = 6 + r % 2
                        tks = []
                        for i in range(4):
                            dt = r * 4 + i
                            tks.append(T(lambda e, dt=dt, i=i, bk=bk, xbuf=xbuf:
                                e.transpose(bank(bk, 128, i * 128),
                                            xbuf[:, dt * 128:(dt + 1) * 128],
                                            idf_s[:]),
                                w=_mx(tf32, tk_bank[bk], tCN) if i == 0 else (),
                                inc=(i == 3)))
                        tk_bankw[bk] = tks[-1]
                        tev = V(lambda e, r=r, tt=tt, bk=bk: e.tensor_copy(
                            big_a32[:, r * 4:(r + 1) * 4, tt * 128:tt * 128 + 128],
                            bank(bk).rearrange("p (a b) -> p a b", a=4)),
                            w=(tks[-1],), inc=True)
                        tev2 = SC(lambda e, r=r, bk=bk: e.activation(
                            fstage[:, r * 512:(r + 1) * 512], bank(bk), AF.Copy),
                            w=_mx(tks[-1], tev,
                                  tk_fstage_free_F if r == 0 else ()),
                            inc=True)
                        tk_bank[bk] = _mx(tev, tev2)
                        tev_all = _mx(tev_all, tev, tev2)
                        xbuf_last_read = _mx(xbuf_last_read, tks[-1])
                    t_ln_done = _mx(t_ln_done, tev_all)
                    if tt % 2 == 0:
                        tk_x1a_free = xbuf_last_read
                    else:
                        tk_x1b_free = xbuf_last_read
                    tks = []
                    for dt in (range(NDT) if f1lvl >= 2 else range(0)):
                        tks.append(T(lambda e, dt=dt: e.matmul(
                            bank(5, 8), fstage[:, dt * 128:(dt + 1) * 128],
                            gw_s[:, dt * 8:(dt + 1) * 8],
                            start=(dt == 0), stop=(dt == NDT - 1)),
                            w=_mx(tev_all, tk_bank[5], tCN) if dt == 0 else (),
                            inc=(dt == NDT - 1)))
                    if f1lvl < 2:
                        continue
                    tk_bankw[5] = tks[-1]
                    tk_fstage_free_F = tks[-1]
                    tsg = SC(lambda e: e.activation(g8[:, 16:24], bank(5, 8),
                                                    AF.Sigmoid),
                             w=_mx(tks[-1], tk_gts_free), inc=True)
                    tk_bank[5] = tsg
                    if f1lvl < 3:
                        tk_gts_free = tsg
                        continue
                    tb = V(lambda e: e.tensor_tensor(g8[:, 24:32], g8[:, 16:24],
                                                     eb_s[:, 0:8], ALU.add),
                           w=_mx(tsg, tEB), inc=True)
                    tm1 = V(lambda e: e.reduce_max(g8[:, 32:33], g8[:, 24:32],
                                                   axis=AX.X), w=(tb,), inc=True)
                    te1 = V(lambda e: e.tensor_scalar(g8[:, 40:48], g8[:, 24:32],
                                                      g8[:, 32:33], None,
                                                      ALU.is_equal),
                            w=(tm1,), inc=True)
                    tr1 = V(lambda e: e.tensor_tensor(g8[:, 40:48], g8[:, 40:48],
                                                      rev_s[:, 0:8], ALU.mult),
                            w=_mx(te1, tCN), inc=True)
                    tmr = V(lambda e: e.reduce_max(g8[:, 33:34], g8[:, 40:48],
                                                   axis=AX.X), w=(tr1,), inc=True)
                    tf1 = V(lambda e: e.tensor_scalar(g8[:, 40:48], g8[:, 40:48],
                                                      g8[:, 33:34], None,
                                                      ALU.is_equal),
                            w=(tmr,), inc=True)
                    tb2 = V(lambda e: e.scalar_tensor_tensor(g8[:, 48:56],
                                                             g8[:, 40:48], -1.0e9,
                                                             g8[:, 24:32],
                                                             ALU.mult, ALU.add),
                            w=(tf1,), inc=True)
                    tm2 = V(lambda e: e.reduce_max(g8[:, 34:35], g8[:, 48:56],
                                                   axis=AX.X), w=(tb2,), inc=True)
                    te2 = V(lambda e: e.tensor_scalar(g8[:, 48:56], g8[:, 48:56],
                                                      g8[:, 34:35], None,
                                                      ALU.is_equal),
                            w=(tm2,), inc=True)
                    tr2 = V(lambda e: e.tensor_tensor(g8[:, 48:56], g8[:, 48:56],
                                                      rev_s[:, 0:8], ALU.mult),
                            w=(te2,), inc=True)
                    tmr2 = V(lambda e: e.reduce_max(g8[:, 35:36], g8[:, 48:56],
                                                    axis=AX.X), w=(tr2,), inc=True)
                    tf2 = V(lambda e: e.tensor_scalar(g8[:, 48:56], g8[:, 48:56],
                                                      g8[:, 35:36], None,
                                                      ALU.is_equal),
                            w=(tmr2,), inc=True)
                    tmsk = V(lambda e: e.tensor_tensor(g8[:, 56:64], g8[:, 40:48],
                                                       g8[:, 48:56], ALU.max),
                             w=(tf2,), inc=True)
                    tgsc = V(lambda e: e.tensor_tensor(g8[:, 56:64], g8[:, 56:64],
                                                       g8[:, 16:24], ALU.mult),
                             w=(tmsk,), inc=True)
                    ttr = T(lambda e: e.transpose(arena_ap[0:8, 2560:2688],
                                                  g8[:, 56:64], idf_s[:]),
                            w=_mx(tgsc, tk_bank[5]), inc=True)
                    tk_bankw[5] = ttr
                    tgv = V(lambda e, tt=tt: e.tensor_copy(
                        gts[:, tt * 128:(tt + 1) * 128], arena_ap[0:8, 2560:2688]),
                        w=_mx(ttr, tk_gts_free), inc=True)
                    tk_bank[5] = tgv
                    t_gts = _mx(t_gts, tgv)

                if dbg_mode == "w13" and c == 0:
                    for _j in range(4):
                        DMA("sync", dbg_out[:, _j * 4096:(_j + 1) * 4096],
                            w13s[8, _j], waits=_mx(t_ln_done))
                if dbg_mode == "ln2" and c == 0:
                    DMA("sync", dbg_out[:, 0:16384], big[:, 0:16384],
                        waits=_mx(t_ln_done))
                t_gdma_prev = ()
                t_acc_last = dict(t_acc_prev)
                elist = (list(range(ne_run)) if not f1_only else [])
                if os.environ.get("MK_ELIST"):
                    elist = [int(t) for t in os.environ["MK_ELIST"].split(",")]
                for eidx, ei in enumerate(elist):
                    routed = ei < E
                    if routed:
                        tgb = []
                        for ch in range(2):
                            bk = 6 + ch
                            tmm = T(lambda e, ei=ei, ch=ch, bk=bk: e.matmul(
                                bank(bk), sel8_s[:, ei * 128:(ei + 1) * 128],
                                gts[0:8, ch * 512:(ch + 1) * 512],
                                start=True, stop=True),
                                w=_mx(t_gts, tk_bank[bk]), inc=True)
                            tk_bankw[bk] = tmm
                            t_gdma_prev = _mx(t_gdma_prev, tmm)
                            tcpg = V(lambda e, ch=ch, bk=bk: e.tensor_copy(
                                qkvr[:, 3072 + ch * 512: 3072 + (ch + 1) * 512],
                                bank(bk)), w=_mx(tmm, tk_gbc_free), inc=True)
                            tk_bank[bk] = tcpg
                            tgb.append(tcpg)
                        tGB = _mx(*tgb)
                        tk_gbc_free = ()
                    else:
                        tGB = ()
                    tw2 = None
                    t_h_done = ()
                    for jt in range(NJT):
                        par = jt % 2
                        w13o = 24576 + par * 4096
                        twd = DMA("sync", big[:, w13o:w13o + 4096], w13s[ei, jt],
                                  waits=_mx(tk_w13buf[par], pump_final,
                                            barrier_A if (c == 0 and eidx == 0 and jt < 2) else ()),
                                  sem="dF%d" % par)
                        if jt == 2:
                            tw2 = DMA("sync", zreg[:, 16384:32768], w2s[ei],
                                      waits=_mx(tk_w2buf, pump_final,
                                                barrier_A if (c == 0 and eidx == 0) else ()),
                                      sem="dW2")
                        t_half = ()
                        for hf in range(2):
                            ubk = hf % 2
                            wk_ = 2 + hf % 2
                            tku = []
                            for dt in range(NDT):
                                tku.append(T(lambda e, dt=dt, hf=hf, ubk=ubk, w13o=w13o:
                                    e.matmul(bank(ubk),
                                             big[:, w13o + dt * 128:
                                                 w13o + (dt + 1) * 128],
                                             big[:, dt * 1024 + hf * 512:
                                                 dt * 1024 + hf * 512 + 512],
                                             start=(dt == 0), stop=(dt == NDT - 1)),
                                    w=_mx(twd, tk_bank[ubk],
                                          t_ln_done) if dt == 0 else (),
                                    inc=(dt == NDT - 1)))
                            tkw = []
                            for dt in range(NDT):
                                tkw.append(T(lambda e, dt=dt, hf=hf, wk_=wk_, w13o=w13o:
                                    e.matmul(bank(wk_),
                                             big[:, w13o + 2048 + dt * 128:
                                                 w13o + 2048 + (dt + 1) * 128],
                                             big[:, dt * 1024 + hf * 512:
                                                 dt * 1024 + hf * 512 + 512],
                                             start=(dt == 0), stop=(dt == NDT - 1)),
                                    w=_mx(tk_bank[wk_],) if dt == 0 else (),
                                    inc=(dt == NDT - 1)))
                            tk_bankw[ubk] = tku[-1]
                            tk_bankw[wk_] = tkw[-1]
                            ho = 16384 + jt * 1024 + hf * 512
                            if routed:
                                tt1 = V(lambda e, ubk=ubk, hf=hf: e.tensor_tensor(
                                    qkvr[:, 0:512], bank(ubk),
                                    qkvr[:, 3072 + hf * 512: 3072 + (hf + 1) * 512],
                                    ALU.mult),
                                    w=_mx(tku[-1], tGB, tk_t123_free), inc=True)
                                ts1 = SC(lambda e: e.activation(
                                    qkvr[:, 1024:1536], qkvr[:, 0:512], AF.Silu),
                                    w=(tt1,), inc=True)
                                tt3 = V(lambda e, wk_=wk_, hf=hf: e.tensor_tensor(
                                    qkvr[:, 2048:2560], bank(wk_),
                                    qkvr[:, 3072 + hf * 512: 3072 + (hf + 1) * 512],
                                    ALU.mult), w=_mx(tkw[-1], tGB), inc=True)
                                th = V(lambda e, ho=ho: e.tensor_tensor(
                                    big[:, ho:ho + 512], qkvr[:, 1024:1536],
                                    qkvr[:, 2048:2560], ALU.mult),
                                    w=_mx(ts1, tt3, tk_h_free), inc=True)
                                tk_bank[ubk] = _mx(tk_bank[ubk], tt1)
                                tk_bank[wk_] = _mx(tk_bank[wk_], tt3)
                                tk_t123_free = th
                            else:
                                ts1 = SC(lambda e, ubk=ubk: e.activation(
                                    qkvr[:, 1024:1536], bank(ubk), AF.Silu),
                                    w=_mx(tku[-1], tk_t123_free), inc=True)
                                th = V(lambda e, ho=ho, wk_=wk_: e.tensor_tensor(
                                    big[:, ho:ho + 512], qkvr[:, 1024:1536],
                                    bank(wk_), ALU.mult),
                                    w=_mx(ts1, tkw[-1], tk_h_free), inc=True)
                                tk_bank[ubk] = _mx(tk_bank[ubk], ts1)
                                tk_bank[wk_] = _mx(tk_bank[wk_], th)
                                tk_t123_free = th
                            t_half = _mx(t_half, th)
                        tk_w13buf[par] = t_half
                        t_h_done = _mx(t_h_done, t_half)
                    for tt in range(8):
                        bks = [0, 1, 2, 3] if tt % 2 == 0 else [4, 5, 6, 7]
                        first_w = _mx(t_h_done, tw2, *[tk_bank[bk] for bk in bks])
                        tks_n = {n: [] for n in range(4)}
                        for jt in range(NJT):
                            for nn in range(4):
                                bk = bks[nn]
                                tks_n[nn].append(T(
                                    lambda e, jt=jt, tt=tt, nn=nn, bk=bk: e.matmul(
                                        bank(bk),
                                        big[:, 16384 + jt * 1024 + tt * 128:
                                            16384 + jt * 1024 + tt * 128 + 128],
                                        zreg[:, 16384 + jt * 2048 + nn * 512:
                                             16384 + jt * 2048 + (nn + 1) * 512],
                                        start=(jt == 0), stop=(jt == NJT - 1)),
                                    w=first_w if (jt == 0 and nn == 0) else (),
                                    inc=(jt == NJT - 1)))
                        evs = []
                        for nn in range(4):
                            bk = bks[nn]
                            tk_bankw[bk] = tks_n[nn][-1]
                            ao = tt * 2048 + nn * 512
                            if eidx == 0:
                                tev = V(lambda e, ao=ao, bk=bk: e.tensor_copy(
                                    zreg[:, ao:ao + 512], bank(bk)),
                                    w=_mx(tks_n[nn][-1], t_acc_last[tt]), inc=True)
                            else:
                                tev = V(lambda e, ao=ao, bk=bk: e.tensor_tensor(
                                    zreg[:, ao:ao + 512], bank(bk),
                                    zreg[:, ao:ao + 512], ALU.add),
                                    w=_mx(tks_n[nn][-1], t_acc_last[tt]), inc=True)
                            tk_bank[bk] = tev
                            evs.append(tev)
                        t_acc_last[tt] = _mx(*evs)
                    tk_h_free = _mx(*[t_acc_last[tt] for tt in range(8)])
                    tk_w2buf = tk_h_free
                    tk_gbc_free = t_h_done

                if dbg_mode == "hacc" and c == 0:
                    DMA("sync", dbg_out[:, 0:8192], big[:, 16384:24576],
                        waits=_mx(*[t_acc_last[tt] for tt in range(8)]))
                    DMA("sync", dbg_out[:, 8192:24576],
                        zreg[:, 0:16384],
                        waits=_mx(*[t_acc_last[tt] for tt in range(8)]))
                for tt in range(8):
                    ti = c * 8 + tt
                    bseq = ti // NT
                    trow_i = ti % NT
                    if tt % 2 == 0:
                        xbuf, xfree = xst, tk_x1a_free
                    else:
                        xbuf, xfree = pmpi, tk_x1b_free
                    wxt = DMA("sync", xbuf[:], x1s[ti],
                              waits=_mx(xfree, barrier_A, spill_final),
                              sem="dX" if tt % 2 == 0 else "dX2")
                    if (ne_run > 0 or os.environ.get("MK_ELIST")) and not f1_only:
                        tadd = V(lambda e, xbuf=xbuf, tt=tt: e.tensor_tensor(
                            xbuf[:], xbuf[:], zreg[:, tt * 2048:(tt + 1) * 2048],
                            ALU.add), w=_mx(wxt, t_acc_last[tt]), inc=True)
                    else:
                        tadd = V(lambda e, xbuf=xbuf: e.tensor_copy(
                            xbuf[:, 0:4], xbuf[:, 0:4]),
                            w=_mx(wxt,), inc=True)
                    tout = DMA("sync",
                               y_out[bseq, trow_i * 128:(trow_i + 1) * 128, :],
                               xbuf[:], waits=(tadd,),
                               sem="dOA" if tt % 2 == 0 else "dOB")
                    if tt % 2 == 0:
                        tk_x1a_free = tout
                    else:
                        tk_x1b_free = tout
                t_acc_prev = {tt: t_acc_last[tt] for tt in range(8)}
                tk_gts_free = _mx(t_gdma_prev)

        final = tuple((s, sch.count[s]) for s in DMAS if sch.count[s] > 0)
        sch.op("sync", lambda e: e.nop(), waits=final)

        sch.check_deadlock()

        @blk.sync
        def _(eng):
            sch.replay("sync", eng, sems)

        @blk.gpsimd
        def _(eng):
            sch.replay("gpsimd", eng, sems)

        @blk.tensor
        def _(eng):
            sch.replay("tensor", eng, sems)

        @blk.vector
        def _(eng):
            sch.replay("vector", eng, sems)

        @blk.scalar
        def _(eng):
            sch.replay("scalar", eng, sems)

    return nc


def _prep_inputs(inputs):
    f32 = np.float32
    WQ = np.asarray(inputs["L0_attn_W_Q"], f32)
    WK = np.asarray(inputs["L0_attn_W_K"], f32)
    WV = np.asarray(inputs["L0_attn_W_V"], f32)
    WO = np.asarray(inputs["L0_attn_W_O"], f32)

    def qkv_tile(W):
        a = W.transpose(1, 0, 2)
        a = a.reshape(NDT, 128, H, DH)
        a = a.transpose(2, 1, 0, 3)
        return np.ascontiguousarray(a)

    wq = qkv_tile(WQ)
    wk = qkv_tile(WK)
    wv = qkv_tile(WV)
    wo = np.ascontiguousarray(WO.reshape(H, 128, D))

    bqkv = np.zeros((128, 48), f32)
    bqkv[:, 0:16] = np.asarray(inputs["L0_attn_b_Q"], f32).T
    bqkv[:, 16:32] = np.asarray(inputs["L0_attn_b_K"], f32).T
    bqkv[:, 32:48] = np.asarray(inputs["L0_attn_b_V"], f32).T
    bo = np.ascontiguousarray(np.asarray(inputs["L0_attn_b_O"], f32).reshape(1, D))

    gwm = np.asarray(inputs["L0_ffn_gate_w"], f32)
    gw = np.ascontiguousarray(gwm.T.reshape(NDT, 128, E).transpose(1, 0, 2)
                              .reshape(128, NDT * E))
    eb = np.ascontiguousarray(np.asarray(inputs["L0_ffn_expert_bias"], f32)
                              .reshape(1, E))

    w1 = np.asarray(inputs["L0_ffn_w1"], f32)
    w3 = np.asarray(inputs["L0_ffn_w3"], f32)
    w2m = np.asarray(inputs["L0_ffn_w2"], f32)
    sw1 = np.asarray(inputs["L0_ffn_sw1"], f32)
    sw2 = np.asarray(inputs["L0_ffn_sw2"], f32)
    sw3 = np.asarray(inputs["L0_ffn_sw3"], f32)

    def w13_tile(w):
        a = w.T.reshape(NDT, 128, NJT, 128)
        return a.transpose(2, 1, 0, 3)

    w1b = np.stack([w13_tile(w1[e]) for e in range(E)] + [w13_tile(sw1)])
    w3b = np.stack([w13_tile(w3[e]) for e in range(E)] + [w13_tile(sw3)])
    w13 = np.ascontiguousarray(
        np.stack([w1b, w3b], axis=3).reshape(NE, NJT, 128, 4096))

    def w2_tile(w):
        a = w.T.reshape(NJT, 128, D)
        return a.transpose(1, 0, 2)

    w2 = np.ascontiguousarray(
        np.stack([w2_tile(w2m[e]) for e in range(E)] + [w2_tile(sw2)])
        .reshape(NE, 128, NJT * D))

    x = np.ascontiguousarray(np.asarray(inputs["L0_attn_x"], f32))
    shared = dict(wq=wq, wk=wk, wv=wv, wo=wo, bqkv=bqkv, bo=bo, gw=gw, eb=eb,
                  w13=w13, w2=w2)
    return x, shared


_NC_CACHE = {}


def kernel(**inputs):
    stage = int(os.environ.get("MK_STAGE", "3"))
    stage_key = (stage, os.environ.get("MK_NE", ""), os.environ.get("MK_F1ONLY", ""),
                 os.environ.get("MK_F1LVL", ""), os.environ.get("MK_ELIST", ""),
                 os.environ.get("MK_DBG", ""))
    x, shared = _prep_inputs(inputs)
    if stage_key not in _NC_CACHE:
        _NC_CACHE[stage_key] = build_program(stage=stage)
    nc = _NC_CACHE[stage_key]
    in_maps = []
    for c in range(NCORES):
        m = dict(shared)
        m["x"] = np.ascontiguousarray(x[c * BPC:(c + 1) * BPC])
        in_maps.append(m)
    res = run_bass_kernel_spmd(nc, in_maps, core_ids=list(range(NCORES)))
    if os.environ.get("MK_DBG"):
        kernel.dbg = [r["dbg"] for r in res.results]
    return np.concatenate([r["y"] for r in res.results], axis=0)


if __name__ == "__main__":
    import time
    t0 = time.time()
    nc = build_program(stage=int(os.environ.get("MK_STAGE", "3")))
    print(f"built ok in {time.time() - t0:.1f}s")
